# version 13
# baseline (speedup 1.0000x reference)
import contextlib

import numpy as np
import ml_dtypes

import concourse.bass as bass
import concourse.mybir as mybir
from concourse.bass_utils import run_bass_kernel_spmd

F32 = mybir.dt.float32
BF16 = mybir.dt.bfloat16
AF = mybir.ActivationFunctionType
ALU = mybir.AluOpType
AX = mybir.AxisListType

D = 2048
T = 2048
TQ = 1024
NH = 16
HD = 128
HF = 8192
NDC = D // 128
NTC = T // 128
NQC = TQ // 128
NHB = HF // 128
NG = 8
SCALE = float(1.0 / np.sqrt(HD))
FW_SCALE = float(1.0 / np.sqrt(HF))
EPS = 1e-5

ENGINES = ("tensor", "vector", "scalar", "gpsimd", "sync")


class Arena:

    def __init__(self, nbytes):
        self.n = nbytes
        self.free_list = [(0, nbytes)]
        self.allocs = {}
        self.hwm = 0

    def alloc(self, name, nbytes, align=64, side="bottom"):
        nbytes = int(nbytes)
        blocks = self.free_list if side == "bottom" else list(reversed(self.free_list))
        for s, sz in blocks:
            if side == "bottom":
                a = (s + align - 1) // align * align
            else:
                a = (s + sz - nbytes) // align * align
                if a < s:
                    continue
            if a + nbytes <= s + sz and a >= s:
                i = self.free_list.index((s, sz))
                pre = (s, a - s)
                post = (a + nbytes, s + sz - (a + nbytes))
                repl = [p for p in (pre, post) if p[1] > 0]
                self.free_list[i:i + 1] = repl
                self.allocs[name] = (a, nbytes)
                self.hwm = max(self.hwm, a + nbytes)
                return a
        raise MemoryError(f"arena OOM for {name} ({nbytes} B); free={self.free_list}")

    def free(self, name):
        a, n = self.allocs.pop(name)
        self.free_list.append((a, n))
        self.free_list.sort()
        merged = []
        for s, sz in self.free_list:
            if merged and merged[-1][0] + merged[-1][1] == s:
                merged[-1] = (merged[-1][0], merged[-1][1] + sz)
            else:
                merged.append((s, sz))
        self.free_list = [tuple(m) for m in merged]


def attn_chunks(j):
    return list(range(0, 4 * (j + 1))) + list(range(8, 16))


def build_nc():
    nc = bass.Bass("TRN2", target_bir_lowering=False)

    x_h = nc.dram_tensor("x", [T, D], F32, kind="ExternalInput").ap()
    wq_h = nc.dram_tensor("wq", [D, D], F32, kind="ExternalInput").ap()
    wk_h = nc.dram_tensor("wk", [D, D], F32, kind="ExternalInput").ap()
    wv_h = nc.dram_tensor("wv", [D, D], F32, kind="ExternalInput").ap()
    wo_h = nc.dram_tensor("wo", [D, D], F32, kind="ExternalInput").ap()
    bq_h = nc.dram_tensor("bq", [128, 16], F32, kind="ExternalInput").ap()
    bk_h = nc.dram_tensor("bk", [128, 16], F32, kind="ExternalInput").ap()
    bo_h = nc.dram_tensor("bo", [128, 16], F32, kind="ExternalInput").ap()
    bvb_h = nc.dram_tensor("bvb", [128, D], F32, kind="ExternalInput").ap()
    w1t_h = nc.dram_tensor("w1t", [D, HF], F32, kind="ExternalInput").ap()
    w3t_h = nc.dram_tensor("w3t", [D, HF], F32, kind="ExternalInput").ap()
    w2t_h = nc.dram_tensor("w2t", [HF, D], F32, kind="ExternalInput").ap()
    mask_h = nc.dram_tensor("maskt", [9 * 128, TQ], BF16, kind="ExternalInput").ap()
    h128_h = nc.dram_tensor("h128", [128, 128], BF16, kind="ExternalInput").ap()
    idb_h = nc.dram_tensor("identb", [128, 128], BF16, kind="ExternalInput").ap()
    out_h = nc.dram_tensor("out", [TQ, D], F32, kind="ExternalOutput").ap()
    x1_hbm = nc.dram_tensor("x1_spill", [TQ, D], F32, kind="Internal").ap()

    ops = {e: [] for e in ENGINES}
    import collections
    cnt = collections.defaultdict(int)
    guard = {}

    def emit(eng, build, waits=(), inc=None, n=1):
        waits = list(waits)
        if eng == "vector":
            if inc is None:
                inc, n = "V", 1
            waits.append(("V", cnt["V"]))
        elif eng == "scalar":
            if inc is None:
                inc, n = "S", 1
            waits.append(("S", cnt["S"]))
        merged = {}
        for s, v in waits:
            if v and v > 0:
                merged[s] = max(merged.get(s, 0), v)
        ops[eng].append((tuple(merged.items()), build, inc, n))
        if inc is not None:
            cnt[inc] += n
            return cnt[inc]
        return None

    def g_wait(name):
        return list(guard.get(name, []))

    def g_set(name, sem, val):
        guard[name] = [(sem, val)]

    def epoch():
        return [("T", cnt["T"]), ("V", cnt["V"]), ("S", cnt["S"]),
                ("SX", cnt["SX"]), ("SW0", cnt["SW0"]), ("SW1", cnt["SW1"]),
                ("SO0", cnt["SO0"]), ("SO1", cnt["SO1"])]

    def preset(names):
        ep = epoch()
        for n in names:
            guard[n] = list(ep)

    es = contextlib.ExitStack()
    with es:
        AR_BYTES = 207 * 1024
        ar_t = es.enter_context(nc.sbuf_tensor("arena", [128, AR_BYTES], mybir.dt.uint8))
        arena_ap = ar_t[:, :]
        ar = Arena(AR_BYTES)

        def aview(name, dtype, elems):
            a, nb = ar.allocs[name]
            return arena_ap[:, a:a + nb].bitcast(dtype)[:, :elems]

        ar.alloc("identb", 256); ar.alloc("h128", 256)
        ar.alloc("bq", 64); ar.alloc("bk", 64); ar.alloc("bo", 64)
        ar.alloc("ones", 64); ar.alloc("eps", 64)
        identb = aview("identb", BF16, 128)
        h128_s = aview("h128", BF16, 128)
        bq_s = aview("bq", F32, 16)
        bk_s = aview("bk", F32, 16)
        bo_s = aview("bo", F32, 16)
        ones_bf = aview("ones", BF16, 1)
        eps_col = aview("eps", F32, 1)

        ar.alloc("bvb", D * 2)
        ar.alloc("recip", 512 * 4); ar.alloc("rb", 512 * 4)
        ar.alloc("onesr", 512)
        ar.alloc("onessq", 128 * 2)
        ar.alloc("st", 16 * 8 * 4)
        bvb_s = aview("bvb", BF16, D)
        recip_sb = aview("recip", F32, 512)[0:1, :]
        rb_sb = aview("rb", F32, 512)
        ones_f32r = aview("onesr", F32, 128)[0:1, :]
        ones_sq = aview("onessq", BF16, 128)
        st1 = aview("st", F32, 16 * 8)
        st2 = st1

        PF = [es.enter_context(nc.psum_tensor(f"pfd{i}", [128, 1024], F32)) for i in range(4)]

        def pf(i):
            return PF[i // 2][:, (i % 2) * 512:(i % 2 + 1) * 512]

        def pbf(i):
            return PF[i // 2][:, :].bitcast(BF16)[:, (i % 2) * 1024:(i % 2 + 1) * 1024]

        sems = {}

        def get_sem(name):
            if name not in sems:
                sems[name] = es.enter_context(nc.semaphore(f"sem_{name}"))
            return sems[name]

        emit("sync", lambda e: e.dma_start(out=identb[:, :], in_=idb_h[:, :]), inc="Lc", n=16)
        emit("sync", lambda e: e.dma_start(out=h128_s[:, :], in_=h128_h[:, :]), inc="Lc", n=16)
        emit("sync", lambda e: e.dma_start(out=bq_s[:, :], in_=bq_h[:, :]), inc="Lc", n=16)
        emit("sync", lambda e: e.dma_start(out=bk_s[:, :], in_=bk_h[:, :]), inc="Lc", n=16)
        L_const = emit("sync", lambda e: e.dma_start(out=bo_s[:, :], in_=bo_h[:, :]), inc="Lc", n=16)
        W_bvb = emit("gpsimd", lambda e: e.dma_start(out=bvb_s[:, :], in_=bvb_h[:, :]),
                     inc="Wc", n=16)
        emit("vector", lambda e: e.memset(ones_bf[:, :], 1.0), inc="V")
        emit("vector", lambda e: e.memset(ones_sq[:, :], 1.0), inc="V")
        emit("vector", lambda e: e.memset(ones_f32r[:, :], 1.0), inc="V")
        V_ones = emit("vector", lambda e: e.memset(eps_col[:, :], EPS), inc="V")

        ar.alloc("mask", 9 * TQ * 2, side="top")
        mask_sb = aview("mask", BF16, 9 * TQ)
        mask3 = mask_sb.rearrange("p (c q) -> p c q", c=9)
        msrc = mask_h.rearrange("(c p) q -> p c q", p=128)
        L_mask = emit("sync",
                      lambda e: e.dma_start(
                          out=mask_sb[:, :].rearrange("p (c q) -> p c q", c=NTC),
                          in_=msrc[:, :, :]),
                      inc="Lm", n=16)

        const_wait = []

        def emit_ln_stage1(tc, src_f32, stats, sidx, xsq_buf, xsqname, src_waits):
            ssum = stats[:, sidx * 8 + 0:sidx * 8 + 1]
            ssq = stats[:, sidx * 8 + 1:sidx * 8 + 2]
            sm = stats[:, sidx * 8 + 2:sidx * 8 + 3]
            smsq = stats[:, sidx * 8 + 3:sidx * 8 + 4]
            svar = stats[:, sidx * 8 + 4:sidx * 8 + 5]
            emit("scalar", lambda e: e.activation(out=xsq_buf[:, :], in_=src_f32[:, :],
                                                  func=AF.Identity, accum_out=ssum),
                 waits=list(src_waits) + g_wait(xsqname), inc="S")
            S_sq = emit("scalar", lambda e: e.activation(out=xsq_buf[:, :], in_=src_f32[:, :],
                                                         func=AF.Square, accum_out=ssq),
                        waits=[], inc="S")
            g_set(xsqname, "S", S_sq)
            emit("vector", lambda e: e.tensor_scalar(out=sm, in0=ssum, scalar1=1.0 / D,
                                                     scalar2=None, op0=ALU.mult),
                 waits=[("S", S_sq)], inc="V")
            emit("vector", lambda e: e.tensor_tensor(out=smsq, in0=sm, in1=sm, op=ALU.mult),
                 inc="V")
            emit("vector", lambda e: e.tensor_scalar(out=svar, in0=ssq, scalar1=1.0 / D,
                                                     scalar2=None, op0=ALU.mult), inc="V")
            V_var = emit("vector", lambda e: e.tensor_tensor(out=svar, in0=svar, in1=smsq,
                                                             op=ALU.subtract), inc="V")
            return V_var

        def emit_ln_stage2(tc, src_f32, stats, sidx, xln_buf, xlnT3_dst, bufname,
                           V_var, dst_guard=None):
            sm = stats[:, sidx * 8 + 2:sidx * 8 + 3]
            ssd = stats[:, sidx * 8 + 5:sidx * 8 + 6]
            srstd = stats[:, sidx * 8 + 6:sidx * 8 + 7]
            svar = stats[:, sidx * 8 + 4:sidx * 8 + 5]
            S_sd = emit("scalar", lambda e: e.activation(out=ssd, in_=svar, func=AF.Sqrt,
                                                         bias=eps_col[:, :]),
                        waits=[("V", V_var)], inc="S")
            emit("vector", lambda e: e.reciprocal(srstd, ssd), waits=[("S", S_sd)], inc="V")
            V_apply = emit("vector",
                           lambda e: e.tensor_scalar(out=xln_buf[:, :], in0=src_f32[:, :],
                                                     scalar1=sm, scalar2=srstd,
                                                     op0=ALU.subtract, op1=ALU.mult),
                           waits=g_wait(bufname), inc="V")
            T_tr = None
            for dc in range(NDC):
                bank = 6 + dc // 8
                slot = dc % 8
                w = ([("V", V_apply)] if dc == 0 else []) +                     (g_wait(f"pbank{bank}") if dc in (0, 8) else []) +                     (list(const_wait) if dc == 0 else [])
                T_tr = emit("tensor",
                            lambda e, _b=bank, _s=slot, _dc=dc: e.transpose(
                                pbf(_b)[:, _s * 128:(_s + 1) * 128],
                                xln_buf[:, _dc * 128:(_dc + 1) * 128], identb[:, :]),
                            waits=w, inc="T")
            g_set(bufname, "T", T_tr)
            V_ev = None
            for dcg in range(4):
                bank = 6 + dcg // 2
                off = (dcg % 2) * 512
                srcp = pbf(bank)[:, off:off + 512].rearrange("p (c t) -> p c t", c=4)
                dst = xlnT3_dst[:, dcg * 4:(dcg + 1) * 4, tc * 128:(tc + 1) * 128]
                V_ev = emit("vector", lambda e, _s=srcp, _d=dst: e.tensor_copy(_d, _s),
                            waits=[("T", T_tr)] +
                                  (g_wait(dst_guard) if dst_guard and tc == 0 and dcg == 0 else []),
                            inc="V")
            g_set("pbank6", "V", V_ev)
            g_set("pbank7", "V", V_ev)
            return V_ev

        ar.alloc("xlnT", NDC * T * 2)
        xlnT = aview("xlnT", BF16, NDC * T)
        xlnT3 = xlnT.rearrange("p (c t) -> p c t", c=NDC)
        for bi in range(4):
            ar.alloc(f"xrow{bi}", D * 4, side="top")
            ar.alloc(f"xln{bi}", D * 2, side="top")
        ar.alloc("xsq0", D * 2, side="top"); ar.alloc("xsq1", D * 2, side="top")
        xrow = [aview(f"xrow{bi}", F32, D) for bi in range(4)]
        xln = [aview(f"xln{bi}", BF16, D) for bi in range(4)]
        xsq = [aview("xsq0", BF16, D), aview("xsq1", BF16, D)]

        const_wait = [("Lc", L_const)]
        V_lnA_done = None
        vvar_a = {}
        for tc in range(NTC):
            L_x = emit("sync",
                       lambda e, _tc=tc: e.dma_start(out=xrow[_tc % 4][:, :],
                                                     in_=x_h[_tc * 128:(_tc + 1) * 128, :]),
                       waits=g_wait(f"xrow{tc % 4}"), inc=f"Lxr{tc % 4}", n=16)
            vvar_a[tc] = emit_ln_stage1(tc, xrow[tc % 4], st1, tc, xsq[tc % 2],
                                        f"xsq{tc % 2}", [(f"Lxr{tc % 4}", L_x)])
            if tc > 0:
                V_lnA_done = emit_ln_stage2(tc - 1, xrow[(tc - 1) % 4], st1, tc - 1,
                                            xln[(tc - 1) % 4], xlnT3, f"xln{(tc - 1) % 4}",
                                            vvar_a[tc - 1])
                g_set(f"xrow{(tc - 1) % 4}", "V", V_lnA_done)
        V_lnA_done = emit_ln_stage2(NTC - 1, xrow[(NTC - 1) % 4], st1, NTC - 1,
                                    xln[(NTC - 1) % 4], xlnT3, f"xln{(NTC - 1) % 4}",
                                    vvar_a[NTC - 1])
        g_set(f"xrow{(NTC - 1) % 4}", "V", V_lnA_done)
        for bi in range(4):
            ar.free(f"xrow{bi}"); ar.free(f"xln{bi}")
        ar.free("xsq0"); ar.free("xsq1")

        preset(["wq_s", "wk_s", "wv_s", "kT_rd", "qT_rd", "v_rd"])
        pt_guard = None
        ar.alloc("wo_s", NDC * 128 * 2 * 4)
        wo_all = aview("wo_s", BF16, NDC * 128 * 4)
        wo_sv = [wo_all[:, i * NDC * 128:(i + 1) * NDC * 128] for i in range(4)]
        ar.alloc("attnT", NDC * TQ * 2)
        attnT = aview("attnT", BF16, NDC * TQ)
        ar.alloc("wq_s", NDC * 256 * 2, side="top"); ar.alloc("wk_s", NDC * 256 * 2, side="top")
        ar.alloc("wv_s", NDC * 256 * 2, side="top")
        wq_s = aview("wq_s", BF16, NDC * 256)
        wk_s = aview("wk_s", BF16, NDC * 256)
        wv_s = aview("wv_s", BF16, NDC * 256)
        ar.alloc("kT_g", 2 * T * 2, side="top"); ar.alloc("qT_g", 2 * TQ * 2, side="top")
        ar.alloc("v_g", NTC * 256 * 2, side="top")
        kT_g = aview("kT_g", BF16, 2 * T)
        qT_g = aview("qT_g", BF16, 2 * TQ)
        v_g = aview("v_g", BF16, NTC * 256)
        ar.alloc("pt", 8 * 1024 * 2, side="top")
        pt = aview("pt", BF16, 8 * 1024)
        pt_guard = [list(epoch()) for _ in range(8)]

        wsrc = {
            "wq": wq_h.rearrange("(dc p) n -> p dc n", p=128),
            "wk": wk_h.rearrange("(dc p) n -> p dc n", p=128),
            "wv": wv_h.rearrange("(dc p) n -> p dc n", p=128),
        }

        deferred_tail = [None]
        for g in range(NG):
            for nm, dstt in (("wq", wq_s), ("wk", wk_s), ("wv", wv_s)):
                W_s = emit("gpsimd",
                           lambda e, _n=nm, _d=dstt, _g=g: e.dma_start(
                               out=_d[:, :].rearrange("p (dc n) -> p dc n", dc=NDC),
                               in_=wsrc[_n][:, :, _g * 256:(_g + 1) * 256]),
                           waits=g_wait(f"{nm}_s"), inc=f"W{nm}", n=16)
                guard[f"{nm}_dma"] = [(f"W{nm}", W_s)]
            for hh in range(2):
                for tb in range(4):
                    bank = (hh * 4 + tb) % 2
                    T_k = None
                    for dc in range(NDC):
                        w = []
                        if dc == 0:
                            w = g_wait(f"pbank{bank}") + guard["wk_dma"]
                            if g == 0 and hh == 0 and tb == 0:
                                w.append(("V", V_lnA_done))
                        T_k = emit("tensor",
                                   lambda e, _b=bank, _dc=dc, _hh=hh, _tb=tb: e.matmul(
                                       pf(_b)[:, :],
                                       wk_s[:, _dc * 256 + _hh * 128:_dc * 256 + (_hh + 1) * 128],
                                       xlnT3[:, _dc, _tb * 512:(_tb + 1) * 512],
                                       start=(_dc == 0), stop=(_dc == NDC - 1)),
                                   waits=w, inc="T")
                    hcol = 2 * g + hh
                    S_ev = emit("scalar",
                                lambda e, _b=bank, _hh=hh, _tb=tb, _hc=hcol: e.activation(
                                    out=kT_g[:, _hh * T + _tb * 512:_hh * T + (_tb + 1) * 512],
                                    in_=pf(_b)[:, :], func=AF.Identity,
                                    bias=bk_s[:, _hc:_hc + 1]),
                                waits=[("T", T_k)] +
                                      (g_wait("kT_rd") if tb == 0 and hh == 0 else []) +
                                      ([("Lc", L_const)] if g == 0 and hh == 0 and tb == 0
                                       else []),
                                inc="S")
                    g_set(f"pbank{bank}", "S", S_ev)
            for hh in range(2):
                for qb in range(2):
                    bank = (hh * 2 + qb) % 2
                    T_q = None
                    for dc in range(NDC):
                        w = []
                        if dc == 0:
                            w = g_wait(f"pbank{bank}") + guard["wq_dma"]
                        T_q = emit("tensor",
                                   lambda e, _b=bank, _dc=dc, _hh=hh, _qb=qb: e.matmul(
                                       pf(_b)[:, :],
                                       wq_s[:, _dc * 256 + _hh * 128:_dc * 256 + (_hh + 1) * 128],
                                       xlnT3[:, _dc, _qb * 512:(_qb + 1) * 512],
                                       start=(_dc == 0), stop=(_dc == NDC - 1)),
                                   waits=w, inc="T")
                    hcol = 2 * g + hh
                    S_ev = emit("scalar",
                                lambda e, _b=bank, _hh=hh, _qb=qb, _hc=hcol: e.activation(
                                    out=qT_g[:, _hh * TQ + _qb * 512:_hh * TQ + (_qb + 1) * 512],
                                    in_=pf(_b)[:, :], func=AF.Identity,
                                    bias=bq_s[:, _hc:_hc + 1]),
                                waits=[("T", T_q)] +
                                      (g_wait("qT_rd") if hh == 0 and qb == 0 else []),
                                inc="S")
                    g_set(f"pbank{bank}", "S", S_ev)
            for tc in range(NTC):
                bank = tc % 2
                T_v = None
                for dc in range(NDC):
                    w = []
                    if dc == 0:
                        w = g_wait(f"pbank{bank}") + guard["wv_dma"]
                    T_v = emit("tensor",
                               lambda e, _b=bank, _dc=dc, _tc=tc: e.matmul(
                                   pf(_b)[:, 0:256],
                                   xlnT3[:, _dc, _tc * 128:(_tc + 1) * 128],
                                   wv_s[:, _dc * 256:(_dc + 1) * 256],
                                   start=(_dc == 0), stop=(_dc == NDC - 1)),
                               waits=w, inc="T")
                V_ev = emit("vector",
                            lambda e, _b=bank, _tc=tc, _g=g: e.tensor_tensor(
                                out=v_g[:, _tc * 256:(_tc + 1) * 256],
                                in0=pf(_b)[:, 0:256],
                                in1=bvb_s[:, _g * 256:(_g + 1) * 256], op=ALU.add),
                            waits=[("T", T_v)] +
                                  ([("Wc", W_bvb)] if g == 0 and tc == 0 else []) +
                                  (g_wait("v_rd") if tc == 0 else []),
                            inc="V")
                g_set(f"pbank{bank}", "V", V_ev)
            for nm in ("wq_s", "wk_s", "wv_s"):
                g_set(nm, "T", cnt["T"])
            ch_set = [set(attn_chunks(j)) for j in range(2)]
            for hh in range(2):
                head = 2 * g + hh
                ready = {}
                T_rs_last = {}
                T_av_last = {}
                pending = []

                def emit_rs_av(c, _hh=hh):
                    for j in (0, 1):
                        if c not in ch_set[j]:
                            continue
                        first = c == 0
                        last = c == 15
                        slot = c % 8
                        wrs = [ready[(c, j)]]
                        if first:
                            wrs += g_wait(f"pbank{4 + j}") + [("V", V_ones)]
                        T_rs_last[j] = emit(
                            "tensor",
                            lambda e, _c=c, _j=j, _s=slot, _f=first, _l=last: e.matmul(
                                PF[2][:, _j * 512:(_j + 1) * 512], ones_sq[:, :],
                                pt[:, _s * 1024 + _j * 512:_s * 1024 + (_j + 1) * 512],
                                start=_f, stop=_l),
                            waits=wrs, inc="T")
                        wav = g_wait(f"pbank{6 + j}") if first else []
                        T_av_last[j] = emit(
                            "tensor",
                            lambda e, _c=c, _j=j, _s=slot, _f=first, _l=last, _h=_hh: e.matmul(
                                PF[3][:, _j * 512:(_j + 1) * 512],
                                v_g[:, _c * 256 + _h * 128:_c * 256 + (_h + 1) * 128],
                                pt[:, _s * 1024 + _j * 512:_s * 1024 + (_j + 1) * 512],
                                start=_f, stop=_l),
                            waits=wav, inc="T")
                    pt_guard[c % 8] = [("T", max(T_av_last.values()))]

                for c in range(16):
                    if c == 2 and deferred_tail[0] is not None:
                        deferred_tail[0]()
                        deferred_tail[0] = None
                    J = [j for j in (0, 1) if c in ch_set[j]]
                    T_lg = 0
                    for j in J:
                        lb = 2 * (c % 2) + j
                        T_lg = max(T_lg, emit(
                            "tensor",
                            lambda e, _lb=lb, _c=c, _hh=hh, _j=j: e.matmul(
                                pf(_lb)[:, :],
                                kT_g[:, _hh * T + _c * 128:_hh * T + (_c + 1) * 128],
                                qT_g[:, _hh * TQ + _j * 512:_hh * TQ + (_j + 1) * 512],
                                start=True, stop=True),
                            waits=g_wait(f"pbank{lb}"), inc="T"))
                    slot = c % 8
                    wexp = [("T", T_lg)] + list(pt_guard[slot])
                    if J == [0, 1]:
                        S_ex = emit("scalar",
                                    lambda e, _c=c, _s=slot: e.activation(
                                        out=pt[:, _s * 1024:(_s + 1) * 1024],
                                        in_=PF[_c % 2][:, :], func=AF.Exp, scale=SCALE),
                                    waits=wexp, inc="S")
                    else:
                        S_ex = emit("scalar",
                                    lambda e, _c=c, _s=slot: e.activation(
                                        out=pt[:, _s * 1024 + 512:_s * 1024 + 1024],
                                        in_=PF[_c % 2][:, 512:1024], func=AF.Exp, scale=SCALE),
                                    waits=wexp, inc="S")
                    for j in J:
                        g_set(f"pbank{2 * (c % 2) + j}", "S", S_ex)
                    mask_j = [j for j in J if c >= 4 * j]
                    wmk = [("S", S_ex)] + ([("Lm", L_mask)] if g == 0 and hh == 0 and c in (0, 8) else [])
                    if mask_j == [0, 1]:
                        V_mk = emit("vector",
                                    lambda e, _c=c, _s=slot: e.tensor_tensor(
                                        out=pt[:, _s * 1024:(_s + 1) * 1024],
                                        in0=pt[:, _s * 1024:(_s + 1) * 1024],
                                        in1=mask3[:, min(_c, 8), :], op=ALU.mult),
                                    waits=wmk, inc="V")
                        for j in J:
                            ready[(c, j)] = ("V", V_mk)
                        pending.append(c)
                        if len(pending) >= 3:
                            emit_rs_av(pending.pop(0))
                        continue
                    else:
                        mj = mask_j[0]
                        V_mk = emit("vector",
                                    lambda e, _c=c, _s=slot, _j=mj: e.tensor_tensor(
                                        out=pt[:, _s * 1024 + _j * 512:_s * 1024 + (_j + 1) * 512],
                                        in0=pt[:, _s * 1024 + _j * 512:_s * 1024 + (_j + 1) * 512],
                                        in1=mask3[:, min(_c, 8), _j * 512:(_j + 1) * 512],
                                        op=ALU.mult),
                                    waits=wmk, inc="V")
                    for j in J:
                        ready[(c, j)] = ("V", V_mk) if j in mask_j else ("S", S_ex)
                    pending.append(c)
                    if len(pending) >= 3:
                        emit_rs_av(pending.pop(0))
                while pending:
                    emit_rs_av(pending.pop(0))

                def make_tail(_head, _rs, _av):
                    def tail():
                        for j in range(2):
                            V_rc = emit("vector",
                                        lambda e, _j=j: e.reciprocal(
                                            rb_sb[:, :],
                                            PF[2][:, _j * 512:(_j + 1) * 512]),
                                        waits=[("T", _rs[j])] + g_wait("rb_sb"), inc="V")
                            g_set(f"pbank{4 + j}", "V", V_rc)
                            V_at = emit("vector",
                                        lambda e, _j=j, _hc=_head: e.tensor_tensor(
                                            out=attnT[:, _hc * TQ + _j * 512:
                                                      _hc * TQ + (_j + 1) * 512],
                                            in0=PF[3][:, _j * 512:(_j + 1) * 512],
                                            in1=rb_sb[:, :], op=ALU.mult),
                                        waits=[("T", _av[j])], inc="V")
                            g_set(f"pbank{6 + j}", "V", V_at)
                            g_set("rb_sb", "V", V_at)
                    return tail

                deferred_tail[0] = make_tail(head, dict(T_rs_last), dict(T_av_last))
            for nm in ("kT_rd", "qT_rd", "v_rd"):
                g_set(nm, "T", cnt["T"])
        if deferred_tail[0] is not None:
            deferred_tail[0]()
            deferred_tail[0] = None
        V_attn_done = cnt["V"]

        ar.free("mask"); ar.free("wq_s"); ar.free("wk_s"); ar.free("wv_s")
        ar.free("kT_g"); ar.free("qT_g"); ar.free("v_g"); ar.free("pt")
        ar.free("xlnT"); ar.free("bvb"); ar.free("recip"); ar.free("rb")

        preset(["oT_w", "x1_w", "xres0", "xres1"])
        ar.alloc("oT", NDC * TQ * 2, side="top")
        oT = aview("oT", BF16, NDC * TQ)
        wo_src = wo_h.rearrange("(hc p) n -> p hc n", p=128)

        ar.alloc("x1", NQC * D * 4)
        x1 = aview("x1", F32, NQC * D)
        ar.alloc("xres0", D * 4, side="top"); ar.alloc("xres1", D * 4, side="top")
        xres = [aview("xres0", F32, D), aview("xres1", F32, D)]
        V_x1 = [None] * NQC
        wo_src2 = wo_h.rearrange("(hc p) n -> p hc n", p=128)
        S_oev = {}
        for j in range(2):
            for doc in range(16):
                buf = (j * 16 + doc) % 4
                W_wo = emit("gpsimd",
                            lambda e, _b=buf, _doc=doc: e.dma_start(
                                out=wo_sv[_b][:, :].rearrange("p (hc n) -> p hc n", hc=NDC),
                                in_=wo_src2[:, :, _doc * 128:(_doc + 1) * 128]),
                            waits=g_wait(f"wo{buf}"), inc=f"Wo{buf}", n=16)
                bank = doc % 2
                T_o = None
                for hc in range(NDC):
                    w = []
                    if hc == 0:
                        w = g_wait(f"pbank{bank}") + [(f"Wo{buf}", W_wo)]
                        if doc == 0 and j == 0:
                            w.append(("V", V_attn_done))
                    T_o = emit("tensor",
                               lambda e, _b=bank, _hc=hc, _j=j, _buf=buf: e.matmul(
                                   pf(_b)[:, :],
                                   wo_sv[_buf][:, _hc * 128:(_hc + 1) * 128],
                                   attnT[:, _hc * TQ + _j * 512:_hc * TQ + (_j + 1) * 512],
                                   start=(_hc == 0), stop=(_hc == NDC - 1)),
                               waits=w, inc="T")
                g_set(f"wo{buf}", "T", T_o)
                S_oev[(j, doc)] = emit(
                    "scalar",
                    lambda e, _b=bank, _doc=doc, _j=j: e.activation(
                        out=oT[:, _doc * TQ + _j * 512:_doc * TQ + (_j + 1) * 512],
                        in_=pf(_b)[:, :], func=AF.Identity, bias=bo_s[:, _doc:_doc + 1]),
                    waits=[("T", T_o)] + (g_wait("oT_w") if doc == 0 and j == 0 else []),
                    inc="S")
                g_set(f"pbank{bank}", "S", S_oev[(j, doc)])
            for tc in range(j * 4, j * 4 + 4):
                L_xr = emit("sync",
                            lambda e, _tc=tc: e.dma_start(
                                out=xres[_tc % 2][:, :],
                                in_=x_h[_tc * 128:(_tc + 1) * 128, :]),
                            waits=g_wait(f"xres{tc % 2}"), inc=f"Lxe{tc % 2}", n=16)
                T_tr = None
                S_need = S_oev[(j, 15)]
                for doc in range(16):
                    bank = 6 + doc // 8
                    slot = doc % 8
                    w = [("S", S_need)] if doc == 0 else []
                    if doc in (0, 8):
                        w += g_wait(f"pbank{bank}")
                    T_tr = emit("tensor",
                                lambda e, _b=bank, _s=slot, _doc=doc, _tc=tc: e.transpose(
                                    pbf(_b)[:, _s * 128:(_s + 1) * 128],
                                    oT[:, _doc * TQ + _tc * 128:_doc * TQ + (_tc + 1) * 128],
                                    identb[:, :]),
                                waits=w, inc="T")
                V_as = None
                for dseg in range(4):
                    bank = 6 + dseg // 2
                    off = (dseg % 2) * 512
                    V_as = emit("vector",
                                lambda e, _b=bank, _o=off, _tc=tc, _d=dseg: e.tensor_tensor(
                                    out=x1[:, _tc * D + _d * 512:_tc * D + (_d + 1) * 512],
                                    in0=xres[_tc % 2][:, _d * 512:(_d + 1) * 512],
                                    in1=pbf(_b)[:, _o:_o + 512], op=ALU.add),
                                waits=[("T", T_tr), (f"Lxe{tc % 2}", L_xr)] +
                                      (g_wait("x1_w") if tc == 0 and dseg == 0 else []),
                                inc="V")
                g_set("pbank6", "V", V_as)
                g_set("pbank7", "V", V_as)
                g_set(f"xres{tc % 2}", "V", V_as)
                V_x1[tc] = V_as
                emit("sync",
                     lambda e, _tc=tc: e.dma_start(
                         out=x1_hbm[_tc * 128:(_tc + 1) * 128, :],
                         in_=x1[:, _tc * D:(_tc + 1) * D]),
                     waits=[("V", V_as)], inc="SX", n=16)
        ST_x1 = cnt["SX"]
        ar.free("wo_s"); ar.free("oT"); ar.free("attnT")
        ar.free("xres0"); ar.free("xres1")

        preset(["xsq0", "xsq1", "xln0", "xln1", "xln2T_w"])
        ar.alloc("xln2T", NDC * TQ * 2)
        xln2T = aview("xln2T", BF16, NDC * TQ)
        xln2T3 = xln2T.rearrange("p (c t) -> p c t", c=NDC)
        ar.alloc("xsq0", D * 2, side="top"); ar.alloc("xsq1", D * 2, side="top")
        ar.alloc("xln0", D * 2, side="top"); ar.alloc("xln1", D * 2, side="top")
        xsq = [aview("xsq0", BF16, D), aview("xsq1", BF16, D)]
        xln2 = [aview("xln0", BF16, D), aview("xln1", BF16, D)]
        vvar_d = {}
        for tc in range(NQC):
            x1c = x1[:, tc * D:(tc + 1) * D]
            vvar_d[tc] = emit_ln_stage1(tc, x1c, st2, tc, xsq[tc % 2], f"xsq{tc % 2}",
                                        [("V", V_x1[tc])])
            if tc > 0:
                emit_ln_stage2(tc - 1, x1[:, (tc - 1) * D:tc * D], st2, tc - 1,
                               xln2[(tc - 1) % 2], xln2T3, f"xln{(tc - 1) % 2}",
                               vvar_d[tc - 1], dst_guard="xln2T_w")
        emit_ln_stage2(NQC - 1, x1[:, (NQC - 1) * D:NQC * D], st2, NQC - 1,
                       xln2[(NQC - 1) % 2], xln2T3, f"xln{(NQC - 1) % 2}",
                       vvar_d[NQC - 1], dst_guard="xln2T_w")
        ar.free("xsq0"); ar.free("xsq1"); ar.free("xln0"); ar.free("xln1")
        ar.free("x1")

        preset(["w1b0", "w1b1", "w1b2", "w3b0", "w3b1", "w3b2", "silu0", "silu1", "hid_w"])
        for _s in range(NHB + 1):
            ar.alloc(f"hid{_s}", TQ * 2)
        _hviews = [aview(f"hid{_s}", BF16, TQ) for _s in range(NHB + 1)]

        def hslot(s, lo, hi):
            return _hviews[s][:, lo:hi]
        ar.alloc("w1_s", NDC * 128 * 2 * 3, side="top"); ar.alloc("w3_s", NDC * 128 * 2 * 3, side="top")
        w1_all = aview("w1_s", BF16, NDC * 128 * 3)
        w3_all = aview("w3_s", BF16, NDC * 128 * 3)
        w1_sv = [w1_all[:, i * NDC * 128:(i + 1) * NDC * 128] for i in range(3)]
        w3_sv = [w3_all[:, i * NDC * 128:(i + 1) * NDC * 128] for i in range(3)]
        ar.alloc("silu0", TQ * 2, side="top"); ar.alloc("silu1", TQ * 2, side="top")
        silu_sb = [aview("silu0", BF16, TQ), aview("silu1", BF16, TQ)]
        w1_src = w1t_h.rearrange("(dc p) n -> p dc n", p=128)
        w3_src = w3t_h.rearrange("(dc p) n -> p dc n", p=128)

        loc = list(range(NHB))
        bf_state = {"spare": NHB}
        slot_last_v = {}

        def bfly_rec(base, size):
            if size == 2:
                return [(base, 1)]
            h = size // 2
            return (bfly_rec(base, h) + bfly_rec(base + h, h) +
                    [(i, h) for i in range(base, base + h)])

        def emit_one_bfly(i, dd):
            a_s, b_s = loc[i], loc[i + dd]
            f_s = bf_state["spare"]
            V1 = emit("vector",
                      lambda e, _a=a_s, _b=b_s, _f=f_s: e.tensor_tensor(
                          out=hslot(_f, 0, TQ), in0=hslot(_a, 0, TQ),
                          in1=hslot(_b, 0, TQ), op=ALU.add),
                      inc="V")
            V2 = emit("vector",
                      lambda e, _a=a_s, _b=b_s: e.tensor_tensor(
                          out=hslot(_a, 0, TQ), in0=hslot(_a, 0, TQ),
                          in1=hslot(_b, 0, TQ), op=ALU.subtract),
                      inc="V")
            slot_last_v[f_s] = V1
            slot_last_v[a_s] = V2
            loc[i], loc[i + dd], bf_state["spare"] = f_s, a_s, b_s

        bf_lower = bfly_rec(0, 32)
        bf_upper = bfly_rec(32, 32)

        for hb in range(NHB):
            buf = hb % 2
            wbuf = hb % 3
            for nm, dst, srcv in (("w1", w1_sv[wbuf], w1_src), ("w3", w3_sv[wbuf], w3_src)):
                W_s = emit("gpsimd",
                           lambda e, _d=dst, _s=srcv, _hb=hb: e.dma_start(
                               out=_d[:, :].rearrange("p (dc n) -> p dc n", dc=NDC),
                               in_=_s[:, :, _hb * 128:(_hb + 1) * 128]),
                           waits=g_wait(f"{nm}b{wbuf}"), inc=f"W{nm}{wbuf}", n=16)
                guard[f"{nm}_dma"] = [(f"W{nm}{wbuf}", W_s)]
            for qb in range(2):
                gate_b = (hb % 2) * 4 + qb
                lin_b = (hb % 2) * 4 + 2 + qb
                T_gate = T_lin = None
                for dc in range(NDC):
                    wg = []
                    wl = []
                    if dc == 0:
                        wg = g_wait(f"pbank{gate_b}") + guard["w1_dma"]
                        wl = g_wait(f"pbank{lin_b}") + guard["w3_dma"]
                    T_gate = emit("tensor",
                                  lambda e, _b=gate_b, _dc=dc, _qb=qb, _buf=wbuf: e.matmul(
                                      pf(_b)[:, :],
                                      w1_sv[_buf][:, _dc * 128:(_dc + 1) * 128],
                                      xln2T3[:, _dc, _qb * 512:(_qb + 1) * 512],
                                      start=(_dc == 0), stop=(_dc == NDC - 1)),
                                  waits=wg, inc="T")
                    T_lin = emit("tensor",
                                 lambda e, _b=lin_b, _dc=dc, _qb=qb, _buf=wbuf: e.matmul(
                                     pf(_b)[:, :],
                                     w3_sv[_buf][:, _dc * 128:(_dc + 1) * 128],
                                     xln2T3[:, _dc, _qb * 512:(_qb + 1) * 512],
                                     start=(_dc == 0), stop=(_dc == NDC - 1)),
                                 waits=wl, inc="T")
                g_set(f"w1b{wbuf}", "T", T_gate)
                g_set(f"w3b{wbuf}", "T", T_lin)
                S_si = emit("scalar",
                            lambda e, _b=gate_b, _buf=buf, _qb=qb: e.activation(
                                out=silu_sb[_buf][:, _qb * 512:(_qb + 1) * 512],
                                in_=pf(_b)[:, :], func=AF.Silu),
                            waits=[("T", T_gate)] +
                                  (g_wait(f"silu{buf}") if qb == 0 else []),
                            inc="S")
                g_set(f"pbank{gate_b}", "S", S_si)
                V_h = emit("vector",
                           lambda e, _b=lin_b, _buf=buf, _qb=qb, _hb=hb: e.tensor_tensor(
                               out=hslot(_hb, _qb * 512, (_qb + 1) * 512),
                               in0=silu_sb[_buf][:, _qb * 512:(_qb + 1) * 512],
                               in1=pf(_b)[:, :], op=ALU.mult),
                           waits=[("S", S_si), ("T", T_lin)] +
                                 (g_wait("hid_w") if hb == 0 and qb == 0 else []),
                           inc="V")
                g_set(f"pbank{lin_b}", "V", V_h)
                if qb == 1:
                    g_set(f"silu{buf}", "V", V_h)
            if hb >= 32:
                emit_one_bfly(hb - 32, 32)
            drained = 0
            while drained < 6:
                if bf_lower and bf_lower[0][0] + bf_lower[0][1] + 32 <= hb:
                    emit_one_bfly(*bf_lower.pop(0)); drained += 1; continue
                if bf_upper and bf_upper[0][0] + bf_upper[0][1] <= hb:
                    emit_one_bfly(*bf_upper.pop(0)); drained += 1; continue
                break
        for lst in (bf_lower, bf_upper):
            while lst:
                emit_one_bfly(*lst.pop(0))
        V_h64_done = cnt["V"]
        ar.free("w1_s"); ar.free("w3_s"); ar.free("silu0"); ar.free("silu1")
        ar.free("xln2T")

        preset(["colh0", "colh1", "colh2", "colh3"])
        ar.alloc("colbuf", 4 * 32 * 128 * 2, side="top")
        colb_all = aview("colbuf", BF16, 4 * 32 * 128)
        colh = [colb_all[:, i * 32 * 128:(i + 1) * 32 * 128] for i in range(4)]
        w2t_src = w2t_h.rearrange("(hb p) n -> p hb n", p=128)

        h128_order = sorted(range(NHB), key=lambda i: slot_last_v.get(loc[i], 0))
        S_h128 = None
        S_h128_chunk = {}
        for pos, i in enumerate(h128_order):
            slot = loc[i]
            T_h1 = None
            for half in range(2):
                bank = 2 * (pos % 2) + half
                T_h1 = emit("tensor",
                            lambda e, _p=pos, _s=slot, _h=half: e.matmul(
                                PF[_p % 2][:, _h * 512:(_h + 1) * 512], h128_s[:, :],
                                hslot(_s, _h * 512, (_h + 1) * 512),
                                start=True, stop=True),
                            waits=g_wait(f"pbank{bank}") +
                                  ([("V", slot_last_v.get(slot, V_h64_done))]
                                   if half == 0 else []),
                            inc="T")
            S_h128 = emit("scalar",
                          lambda e, _p=pos, _s=slot: e.activation(
                              out=hslot(_s, 0, TQ), in_=PF[_p % 2][:, :],
                              func=AF.Identity, scale=FW_SCALE),
                          waits=[("T", T_h1)], inc="S")
            for half in range(2):
                g_set(f"pbank{2 * (pos % 2) + half}", "S", S_h128)
            S_h128_chunk[i] = S_h128
        S_h128_done = S_h128

        preset(["o2T", "outb0", "outb1"])
        ar.alloc("o2T", 16 * 512 * 2)
        o2T = aview("o2T", BF16, 16 * 512)
        ar.alloc("outb0", D * 4, side="top"); ar.alloc("outb1", D * 4, side="top")
        out_sb = [aview("outb0", F32, D), aview("outb1", F32, D)]

        for qb in range(2):
            S_ev_last = None
            for doc in range(16):
                L_half = [None, None]
                for half in range(2):
                    hbuf = (2 * doc + half) % 4
                    L_half[half] = emit(
                        "gpsimd",
                        lambda e, _hb=hbuf, _doc=doc, _h=half: e.dma_start(
                            out=colh[_hb][:, :].rearrange("p (hb t) -> p hb t", hb=32),
                            in_=w2t_src[:, _h * 32:(_h + 1) * 32,
                                        _doc * 128:(_doc + 1) * 128]),
                        waits=g_wait(f"colh{hbuf}"), inc=f"Lch{hbuf}", n=16)
                bank = doc % 2
                T_d = None
                for hb in range(NHB):
                    w = []
                    if hb == 0:
                        w = g_wait(f"pbank{bank}") + [(f"Lch{(2 * doc) % 4}", L_half[0]),
                                                       ("S", max(S_h128_chunk[i]
                                                                 for i in range(32)))]
                    elif hb == 32:
                        w = [(f"Lch{(2 * doc + 1) % 4}", L_half[1]),
                             ("S", max(S_h128_chunk[i] for i in range(NHB)))]
                    slot = loc[hb]
                    hbuf = (2 * doc + hb // 32) % 4
                    T_d = emit("tensor",
                               lambda e, _b=bank, _hb=hb, _buf=hbuf, _s=slot, _qb=qb: e.matmul(
                                   pf(_b)[:, :],
                                   colh[_buf][:, (_hb % 32) * 128:((_hb % 32) + 1) * 128],
                                   hslot(_s, _qb * 512, (_qb + 1) * 512),
                                   start=(_hb == 0), stop=(_hb == NHB - 1)),
                               waits=w, inc="T")
                    if hb == 31:
                        g_set(f"colh{(2 * doc) % 4}", "T", cnt["T"])
                g_set(f"colh{(2 * doc + 1) % 4}", "T", T_d)
                S_ev_last = emit("scalar",
                                 lambda e, _b=bank, _doc=doc: e.activation(
                                     out=o2T[:, _doc * 512:(_doc + 1) * 512],
                                     in_=pf(_b)[:, :], func=AF.Copy),
                                 waits=[("T", T_d)] + (g_wait("o2T") if doc == 0 else []),
                                 inc="S")
                g_set(f"pbank{bank}", "S", S_ev_last)
            T_tr_last = None
            for tc in range(4):
                buf = tc % 2
                L_x1 = emit("sync",
                            lambda e, _b=buf, _tc=tc, _qb=qb: e.dma_start(
                                out=out_sb[_b][:, :],
                                in_=x1_hbm[_qb * 512 + _tc * 128:
                                           _qb * 512 + (_tc + 1) * 128, :]),
                            waits=[("SX", ST_x1)] + g_wait(f"outb{buf}"), inc=f"Lo{buf}", n=16)
                T_tr = None
                for doc in range(16):
                    bank = 6 + doc // 8
                    slot = doc % 8
                    w = [("S", S_ev_last)] if doc == 0 and tc == 0 else []
                    if doc in (0, 8):
                        w += g_wait(f"pbank{bank}")
                    T_tr = emit("tensor",
                                lambda e, _b=bank, _s=slot, _doc=doc, _tc=tc: e.transpose(
                                    pbf(_b)[:, _s * 128:(_s + 1) * 128],
                                    o2T[:, _doc * 512 + _tc * 128:_doc * 512 + (_tc + 1) * 128],
                                    identb[:, :]),
                                waits=w, inc="T")
                T_tr_last = T_tr
                V_ad = None
                for dseg in range(4):
                    bank = 6 + dseg // 2
                    off = (dseg % 2) * 512
                    V_ad = emit("vector",
                                lambda e, _b=bank, _o=off, _d=dseg, _buf=buf: e.tensor_tensor(
                                    out=out_sb[_buf][:, _d * 512:(_d + 1) * 512],
                                    in0=out_sb[_buf][:, _d * 512:(_d + 1) * 512],
                                    in1=pbf(_b)[:, _o:_o + 512], op=ALU.add),
                                waits=[("T", T_tr), (f"Lo{buf}", L_x1)], inc="V")
                g_set("pbank6", "V", V_ad)
                g_set("pbank7", "V", V_ad)
                ST_o = emit("sync",
                            lambda e, _b=buf, _tc=tc, _qb=qb: e.dma_start(
                                out=out_h[_qb * 512 + _tc * 128:
                                          _qb * 512 + (_tc + 1) * 128, :],
                                in_=out_sb[_b][:, :]),
                            waits=[("V", V_ad)], inc=f"SO{buf}", n=16)
                g_set(f"outb{buf}", f"SO{buf}", ST_o)
            g_set("o2T", "T", T_tr_last)

        emit("sync", lambda e: None,
             waits=[("SO0", cnt["SO0"]), ("SO1", cnt["SO1"])])

        assert ar.hwm <= AR_BYTES, ar.hwm

        import os as _os
        lazy_on = _os.environ.get("KLAZY", "1") == "1"
        needed_t = set()
        for _e in ENGINES:
            for _w, _b, _i, _n in ops[_e]:
                for _s, _v in _w:
                    if _s == "T":
                        needed_t.add(_v)
        new_tops = []
        run_v = 0
        last_v = 0
        for _w, _b, _i, _n in ops["tensor"]:
            if _i == "T":
                run_v += _n
                if run_v in needed_t:
                    new_tops.append((_w, _b, "T", run_v - last_v))
                    last_v = run_v
                else:
                    new_tops.append((_w, _b, None, 0))
            else:
                new_tops.append((_w, _b, _i, _n))
        if run_v != last_v:
            for idx in range(len(new_tops) - 1, -1, -1):
                if new_tops[idx][2] is None and ops["tensor"][idx][2] == "T":
                    new_tops[idx] = (new_tops[idx][0], new_tops[idx][1], "T", run_v - last_v)
                    break
        if lazy_on:
            ops["tensor"] = new_tops

        block = es.enter_context(nc.Block())

        def mk_runner(eng_ops):
            def run(e):
                for waits, build, inc, n in eng_ops:
                    for s, v in waits:
                        e.wait_ge(get_sem(s), v)
                    ins = build(e)
                    if inc is not None:
                        ins.then_inc(get_sem(inc), n)
            return run

        block.tensor(mk_runner(ops["tensor"]))
        block.vector(mk_runner(ops["vector"]))
        block.scalar(mk_runner(ops["scalar"]))
        block.gpsimd(mk_runner(ops["gpsimd"]))
        block.sync(mk_runner(ops["sync"]))

    return nc


_NC_CACHE = None


def _get_nc():
    global _NC_CACHE
    if _NC_CACHE is None:
        _NC_CACHE = build_nc()
    return _NC_CACHE


def _hadamard(n):
    h = np.array([[1.0]], dtype=np.float32)
    while h.shape[0] < n:
        h = np.block([[h, h], [h, -h]])
    return h.astype(np.float32)


def prep_in_maps(L0_attn_query, L0_attn_w_q, L0_attn_b_q, L0_attn_w_k, L0_attn_b_k,
                 L0_attn_w_v, L0_attn_b_v, L0_attn_w_o, L0_attn_b_o,
                 L0_ffn_W1, L0_ffn_W2, L0_ffn_W3):
    bf = ml_dtypes.bfloat16
    x = np.ascontiguousarray(np.asarray(L0_attn_query, dtype=np.float32))
    wq = np.ascontiguousarray(np.asarray(L0_attn_w_q, dtype=np.float32))
    wk = np.ascontiguousarray(np.asarray(L0_attn_w_k, dtype=np.float32))
    wv = np.ascontiguousarray(np.asarray(L0_attn_w_v, dtype=np.float32))
    wo = np.ascontiguousarray(np.asarray(L0_attn_w_o, dtype=np.float32))
    w1t = np.ascontiguousarray(np.asarray(L0_ffn_W1, dtype=np.float32).T)
    w3t = np.ascontiguousarray(np.asarray(L0_ffn_W3, dtype=np.float32).T)
    w2t = np.ascontiguousarray(np.asarray(L0_ffn_W2, dtype=np.float32).T)
    bq = np.ascontiguousarray(np.asarray(L0_attn_b_q, dtype=np.float32).reshape(16, 128).T)
    bk = np.ascontiguousarray(np.asarray(L0_attn_b_k, dtype=np.float32).reshape(16, 128).T)
    bo = np.ascontiguousarray(np.asarray(L0_attn_b_o, dtype=np.float32).reshape(16, 128).T)
    bvb = np.ascontiguousarray(np.broadcast_to(
        np.asarray(L0_attn_b_v, dtype=np.float32).reshape(1, D), (128, D)))
    h128 = _hadamard(128).astype(bf)
    identb = np.eye(128, dtype=np.float32).astype(bf)

    masks = []
    for half in range(2):
        q = half * TQ + np.arange(TQ)[None, :]
        k = (np.arange(9 * 128)[:, None] + half * TQ) % T
        m = (q >= k).astype(np.float32)
        masks.append(np.ascontiguousarray(m.astype(bf)))

    in_maps = []
    for c in range(8):
        b, half = c // 2, c % 2
        xc = np.ascontiguousarray(np.roll(x[b], -half * TQ, axis=0))
        in_maps.append(dict(
            x=xc, wq=wq, wk=wk, wv=wv, wo=wo, bq=bq, bk=bk, bo=bo, bvb=bvb,
            w1t=w1t, w3t=w3t, w2t=w2t, maskt=masks[half], h128=h128, identb=identb,
        ))
    return in_maps


def kernel(**inputs):
    in_maps = prep_in_maps(**inputs)
    nc = _get_nc()
    res = run_bass_kernel_spmd(nc, in_maps, core_ids=list(range(8)))
    out = np.empty((4, T, D), dtype=np.float32)
    for c in range(8):
        b, half = c // 2, c % 2
        out[b, half * TQ:(half + 1) * TQ] = res.results[c]["out"]
    return out


# version 14
# speedup vs baseline: 1.0623x; 1.0623x over previous
import contextlib

import numpy as np
import ml_dtypes

import concourse.bass as bass
import concourse.mybir as mybir
from concourse.bass_utils import run_bass_kernel_spmd

F32 = mybir.dt.float32
BF16 = mybir.dt.bfloat16
AF = mybir.ActivationFunctionType
ALU = mybir.AluOpType
AX = mybir.AxisListType

D = 2048
T = 2048
TQ = 1024
NH = 16
HD = 128
HF = 8192
NDC = D // 128
NTC = T // 128
NQC = TQ // 128
NHB = HF // 128
NG = 8
SCALE = float(1.0 / np.sqrt(HD))
FW_SCALE = float(1.0 / np.sqrt(HF))
EPS = 1e-5

ENGINES = ("tensor", "vector", "scalar", "gpsimd", "sync")


class Arena:

    def __init__(self, nbytes):
        self.n = nbytes
        self.free_list = [(0, nbytes)]
        self.allocs = {}
        self.hwm = 0

    def alloc(self, name, nbytes, align=64, side="bottom"):
        nbytes = int(nbytes)
        blocks = self.free_list if side == "bottom" else list(reversed(self.free_list))
        for s, sz in blocks:
            if side == "bottom":
                a = (s + align - 1) // align * align
            else:
                a = (s + sz - nbytes) // align * align
                if a < s:
                    continue
            if a + nbytes <= s + sz and a >= s:
                i = self.free_list.index((s, sz))
                pre = (s, a - s)
                post = (a + nbytes, s + sz - (a + nbytes))
                repl = [p for p in (pre, post) if p[1] > 0]
                self.free_list[i:i + 1] = repl
                self.allocs[name] = (a, nbytes)
                self.hwm = max(self.hwm, a + nbytes)
                return a
        raise MemoryError(f"arena OOM for {name} ({nbytes} B); free={self.free_list}")

    def free(self, name):
        a, n = self.allocs.pop(name)
        self.free_list.append((a, n))
        self.free_list.sort()
        merged = []
        for s, sz in self.free_list:
            if merged and merged[-1][0] + merged[-1][1] == s:
                merged[-1] = (merged[-1][0], merged[-1][1] + sz)
            else:
                merged.append((s, sz))
        self.free_list = [tuple(m) for m in merged]


def attn_chunks(j):
    return list(range(0, 4 * (j + 1))) + list(range(8, 16))


def build_nc():
    nc = bass.Bass("TRN2", target_bir_lowering=False)

    x_h = nc.dram_tensor("x", [T, D], F32, kind="ExternalInput").ap()
    wq_h = nc.dram_tensor("wq", [D, D], F32, kind="ExternalInput").ap()
    wk_h = nc.dram_tensor("wk", [D, D], F32, kind="ExternalInput").ap()
    wv_h = nc.dram_tensor("wv", [D, D], F32, kind="ExternalInput").ap()
    wo_h = nc.dram_tensor("wo", [D, D], F32, kind="ExternalInput").ap()
    bq_h = nc.dram_tensor("bq", [128, 16], F32, kind="ExternalInput").ap()
    bk_h = nc.dram_tensor("bk", [128, 16], F32, kind="ExternalInput").ap()
    bo_h = nc.dram_tensor("bo", [128, 16], F32, kind="ExternalInput").ap()
    bvb_h = nc.dram_tensor("bvb", [128, D], F32, kind="ExternalInput").ap()
    w1t_h = nc.dram_tensor("w1t", [D, HF], F32, kind="ExternalInput").ap()
    w3t_h = nc.dram_tensor("w3t", [D, HF], F32, kind="ExternalInput").ap()
    w2t_h = nc.dram_tensor("w2t", [HF, D], F32, kind="ExternalInput").ap()
    mask_h = nc.dram_tensor("maskt", [9 * 128, TQ], BF16, kind="ExternalInput").ap()
    h128_h = nc.dram_tensor("h128", [128, 128], BF16, kind="ExternalInput").ap()
    idb_h = nc.dram_tensor("identb", [128, 128], BF16, kind="ExternalInput").ap()
    out_h = nc.dram_tensor("out", [TQ, D], F32, kind="ExternalOutput").ap()
    x1_hbm = nc.dram_tensor("x1_spill", [TQ, D], F32, kind="Internal").ap()

    ops = {e: [] for e in ENGINES}
    import collections
    cnt = collections.defaultdict(int)
    guard = {}

    def emit(eng, build, waits=(), inc=None, n=1):
        waits = list(waits)
        if eng == "vector":
            if inc is None:
                inc, n = "V", 1
            waits.append(("V", cnt["V"]))
        elif eng == "scalar":
            if inc is None:
                inc, n = "S", 1
            waits.append(("S", cnt["S"]))
        merged = {}
        for s, v in waits:
            if v and v > 0:
                merged[s] = max(merged.get(s, 0), v)
        ops[eng].append((tuple(merged.items()), build, inc, n))
        if inc is not None:
            cnt[inc] += n
            return cnt[inc]
        return None

    def g_wait(name):
        return list(guard.get(name, []))

    def g_set(name, sem, val):
        guard[name] = [(sem, val)]

    def epoch():
        return [("T", cnt["T"]), ("V", cnt["V"]), ("S", cnt["S"]),
                ("SX", cnt["SX"]), ("SW0", cnt["SW0"]), ("SW1", cnt["SW1"]),
                ("SO0", cnt["SO0"]), ("SO1", cnt["SO1"])]

    def preset(names):
        ep = epoch()
        for n in names:
            guard[n] = list(ep)

    es = contextlib.ExitStack()
    with es:
        AR_BYTES = 207 * 1024
        ar_t = es.enter_context(nc.sbuf_tensor("arena", [128, AR_BYTES], mybir.dt.uint8))
        arena_ap = ar_t[:, :]
        ar = Arena(AR_BYTES)

        def aview(name, dtype, elems):
            a, nb = ar.allocs[name]
            return arena_ap[:, a:a + nb].bitcast(dtype)[:, :elems]

        ar.alloc("identb", 256); ar.alloc("h128", 256)
        ar.alloc("bq", 64); ar.alloc("bk", 64); ar.alloc("bo", 64)
        ar.alloc("ones", 64); ar.alloc("eps", 64)
        identb = aview("identb", BF16, 128)
        h128_s = aview("h128", BF16, 128)
        bq_s = aview("bq", F32, 16)
        bk_s = aview("bk", F32, 16)
        bo_s = aview("bo", F32, 16)
        ones_bf = aview("ones", BF16, 1)
        eps_col = aview("eps", F32, 1)

        ar.alloc("bvb", D * 2)
        ar.alloc("recip", 512 * 4); ar.alloc("rb", 512 * 4)
        ar.alloc("onesr", 512)
        ar.alloc("onessq", 128 * 2)
        ar.alloc("st", 16 * 8 * 4)
        bvb_s = aview("bvb", BF16, D)
        recip_sb = aview("recip", F32, 512)[0:1, :]
        rb_sb = aview("rb", F32, 512)
        ones_f32r = aview("onesr", F32, 128)[0:1, :]
        ones_sq = aview("onessq", BF16, 128)
        st1 = aview("st", F32, 16 * 8)
        st2 = st1

        PF = [es.enter_context(nc.psum_tensor(f"pfd{i}", [128, 1024], F32)) for i in range(4)]

        def pf(i):
            return PF[i // 2][:, (i % 2) * 512:(i % 2 + 1) * 512]

        def pbf(i):
            return PF[i // 2][:, :].bitcast(BF16)[:, (i % 2) * 1024:(i % 2 + 1) * 1024]

        sems = {}

        def get_sem(name):
            if name not in sems:
                sems[name] = es.enter_context(nc.semaphore(f"sem_{name}"))
            return sems[name]

        emit("sync", lambda e: e.dma_start(out=identb[:, :], in_=idb_h[:, :]), inc="Lc", n=16)
        emit("sync", lambda e: e.dma_start(out=h128_s[:, :], in_=h128_h[:, :]), inc="Lc", n=16)
        emit("sync", lambda e: e.dma_start(out=bq_s[:, :], in_=bq_h[:, :]), inc="Lc", n=16)
        emit("sync", lambda e: e.dma_start(out=bk_s[:, :], in_=bk_h[:, :]), inc="Lc", n=16)
        L_const = emit("sync", lambda e: e.dma_start(out=bo_s[:, :], in_=bo_h[:, :]), inc="Lc", n=16)
        W_bvb = emit("gpsimd", lambda e: e.dma_start(out=bvb_s[:, :], in_=bvb_h[:, :]),
                     inc="Wc", n=16)
        emit("vector", lambda e: e.memset(ones_bf[:, :], 1.0), inc="V")
        emit("vector", lambda e: e.memset(ones_sq[:, :], 1.0), inc="V")
        emit("vector", lambda e: e.memset(ones_f32r[:, :], 1.0), inc="V")
        V_ones = emit("vector", lambda e: e.memset(eps_col[:, :], EPS), inc="V")

        ar.alloc("mask", 9 * TQ * 2, side="top")
        mask_sb = aview("mask", BF16, 9 * TQ)
        mask3 = mask_sb.rearrange("p (c q) -> p c q", c=9)
        msrc = mask_h.rearrange("(c p) q -> p c q", p=128)
        L_mask = emit("sync",
                      lambda e: e.dma_start(
                          out=mask_sb[:, :].rearrange("p (c q) -> p c q", c=NTC),
                          in_=msrc[:, :, :]),
                      inc="Lm", n=16)

        const_wait = []

        def emit_ln_stage1(tc, src_f32, stats, sidx, xsq_buf, xsqname, src_waits):
            ssum = stats[:, sidx * 8 + 0:sidx * 8 + 1]
            ssq = stats[:, sidx * 8 + 1:sidx * 8 + 2]
            sm = stats[:, sidx * 8 + 2:sidx * 8 + 3]
            smsq = stats[:, sidx * 8 + 3:sidx * 8 + 4]
            svar = stats[:, sidx * 8 + 4:sidx * 8 + 5]
            emit("scalar", lambda e: e.activation(out=xsq_buf[:, :], in_=src_f32[:, :],
                                                  func=AF.Identity, accum_out=ssum),
                 waits=list(src_waits) + g_wait(xsqname), inc="S")
            S_sq = emit("scalar", lambda e: e.activation(out=xsq_buf[:, :], in_=src_f32[:, :],
                                                         func=AF.Square, accum_out=ssq),
                        waits=[], inc="S")
            g_set(xsqname, "S", S_sq)
            emit("vector", lambda e: e.tensor_scalar(out=sm, in0=ssum, scalar1=1.0 / D,
                                                     scalar2=None, op0=ALU.mult),
                 waits=[("S", S_sq)], inc="V")
            emit("vector", lambda e: e.tensor_tensor(out=smsq, in0=sm, in1=sm, op=ALU.mult),
                 inc="V")
            emit("vector", lambda e: e.tensor_scalar(out=svar, in0=ssq, scalar1=1.0 / D,
                                                     scalar2=None, op0=ALU.mult), inc="V")
            V_var = emit("vector", lambda e: e.tensor_tensor(out=svar, in0=svar, in1=smsq,
                                                             op=ALU.subtract), inc="V")
            return V_var

        def emit_ln_stage2(tc, src_f32, stats, sidx, xln_buf, xlnT3_dst, bufname,
                           V_var, dst_guard=None):
            sm = stats[:, sidx * 8 + 2:sidx * 8 + 3]
            ssd = stats[:, sidx * 8 + 5:sidx * 8 + 6]
            srstd = stats[:, sidx * 8 + 6:sidx * 8 + 7]
            svar = stats[:, sidx * 8 + 4:sidx * 8 + 5]
            S_sd = emit("scalar", lambda e: e.activation(out=ssd, in_=svar, func=AF.Sqrt,
                                                         bias=eps_col[:, :]),
                        waits=[("V", V_var)], inc="S")
            emit("vector", lambda e: e.reciprocal(srstd, ssd), waits=[("S", S_sd)], inc="V")
            V_apply = emit("vector",
                           lambda e: e.tensor_scalar(out=xln_buf[:, :], in0=src_f32[:, :],
                                                     scalar1=sm, scalar2=srstd,
                                                     op0=ALU.subtract, op1=ALU.mult),
                           waits=g_wait(bufname), inc="V")
            T_tr = None
            for dc in range(NDC):
                bank = 6 + dc // 8
                slot = dc % 8
                w = ([("V", V_apply)] if dc == 0 else []) +                     (g_wait(f"pbank{bank}") if dc in (0, 8) else []) +                     (list(const_wait) if dc == 0 else [])
                T_tr = emit("tensor",
                            lambda e, _b=bank, _s=slot, _dc=dc: e.transpose(
                                pbf(_b)[:, _s * 128:(_s + 1) * 128],
                                xln_buf[:, _dc * 128:(_dc + 1) * 128], identb[:, :]),
                            waits=w, inc="T")
            g_set(bufname, "T", T_tr)
            V_ev = None
            for dcg in range(4):
                bank = 6 + dcg // 2
                off = (dcg % 2) * 512
                srcp = pbf(bank)[:, off:off + 512].rearrange("p (c t) -> p c t", c=4)
                dst = xlnT3_dst[:, dcg * 4:(dcg + 1) * 4, tc * 128:(tc + 1) * 128]
                V_ev = emit("vector", lambda e, _s=srcp, _d=dst: e.tensor_copy(_d, _s),
                            waits=[("T", T_tr)] +
                                  (g_wait(dst_guard) if dst_guard and tc == 0 and dcg == 0 else []),
                            inc="V")
            g_set("pbank6", "V", V_ev)
            g_set("pbank7", "V", V_ev)
            return V_ev

        ar.alloc("xlnT", NDC * T * 2)
        xlnT = aview("xlnT", BF16, NDC * T)
        xlnT3 = xlnT.rearrange("p (c t) -> p c t", c=NDC)
        for bi in range(4):
            ar.alloc(f"xrow{bi}", D * 4, side="top")
            ar.alloc(f"xln{bi}", D * 2, side="top")
        ar.alloc("xsq0", D * 2, side="top"); ar.alloc("xsq1", D * 2, side="top")
        xrow = [aview(f"xrow{bi}", F32, D) for bi in range(4)]
        xln = [aview(f"xln{bi}", BF16, D) for bi in range(4)]
        xsq = [aview("xsq0", BF16, D), aview("xsq1", BF16, D)]

        const_wait = [("Lc", L_const)]
        V_lnA_done = None
        V_ln_chunk = {}
        vvar_a = {}
        for tc in range(NTC):
            L_x = emit("sync",
                       lambda e, _tc=tc: e.dma_start(out=xrow[_tc % 4][:, :],
                                                     in_=x_h[_tc * 128:(_tc + 1) * 128, :]),
                       waits=g_wait(f"xrow{tc % 4}"), inc=f"Lxr{tc % 4}", n=16)
            vvar_a[tc] = emit_ln_stage1(tc, xrow[tc % 4], st1, tc, xsq[tc % 2],
                                        f"xsq{tc % 2}", [(f"Lxr{tc % 4}", L_x)])
            if tc > 0:
                V_lnA_done = emit_ln_stage2(tc - 1, xrow[(tc - 1) % 4], st1, tc - 1,
                                            xln[(tc - 1) % 4], xlnT3, f"xln{(tc - 1) % 4}",
                                            vvar_a[tc - 1])
                V_ln_chunk[tc - 1] = V_lnA_done
                g_set(f"xrow{(tc - 1) % 4}", "V", V_lnA_done)
        V_lnA_done = emit_ln_stage2(NTC - 1, xrow[(NTC - 1) % 4], st1, NTC - 1,
                                    xln[(NTC - 1) % 4], xlnT3, f"xln{(NTC - 1) % 4}",
                                    vvar_a[NTC - 1])
        V_ln_chunk[NTC - 1] = V_lnA_done
        g_set(f"xrow{(NTC - 1) % 4}", "V", V_lnA_done)
        for bi in range(4):
            ar.free(f"xrow{bi}"); ar.free(f"xln{bi}")
        ar.free("xsq0"); ar.free("xsq1")

        preset(["wq_s", "wk_s", "wv_s", "kT_rd", "qT_rd", "v_rd"])
        pt_guard = None
        ar.alloc("wo_s", NDC * 128 * 2 * 4)
        wo_all = aview("wo_s", BF16, NDC * 128 * 4)
        wo_sv = [wo_all[:, i * NDC * 128:(i + 1) * NDC * 128] for i in range(4)]
        ar.alloc("attnT", NDC * TQ * 2)
        attnT = aview("attnT", BF16, NDC * TQ)
        ar.alloc("wq_s", NDC * 256 * 2, side="top"); ar.alloc("wk_s", NDC * 256 * 2, side="top")
        ar.alloc("wv_s", NDC * 256 * 2, side="top")
        wq_s = aview("wq_s", BF16, NDC * 256)
        wk_s = aview("wk_s", BF16, NDC * 256)
        wv_s = aview("wv_s", BF16, NDC * 256)
        ar.alloc("kT_g", 2 * T * 2, side="top"); ar.alloc("qT_g", 2 * TQ * 2, side="top")
        ar.alloc("v_g", NTC * 256 * 2, side="top")
        kT_g = aview("kT_g", BF16, 2 * T)
        qT_g = aview("qT_g", BF16, 2 * TQ)
        v_g = aview("v_g", BF16, NTC * 256)
        ar.alloc("pt", 8 * 1024 * 2, side="top")
        pt = aview("pt", BF16, 8 * 1024)
        pt_guard = [list(epoch()) for _ in range(8)]

        wsrc = {
            "wq": wq_h.rearrange("(dc p) n -> p dc n", p=128),
            "wk": wk_h.rearrange("(dc p) n -> p dc n", p=128),
            "wv": wv_h.rearrange("(dc p) n -> p dc n", p=128),
        }

        deferred_tail = [None]
        for g in range(NG):
            for nm, dstt in (("wq", wq_s), ("wk", wk_s), ("wv", wv_s)):
                W_s = emit("gpsimd",
                           lambda e, _n=nm, _d=dstt, _g=g: e.dma_start(
                               out=_d[:, :].rearrange("p (dc n) -> p dc n", dc=NDC),
                               in_=wsrc[_n][:, :, _g * 256:(_g + 1) * 256]),
                           waits=g_wait(f"{nm}_s"), inc=f"W{nm}", n=16)
                guard[f"{nm}_dma"] = [(f"W{nm}", W_s)]
            for hh in range(2):
                for tb in range(4):
                    bank = (hh * 4 + tb) % 2
                    T_k = None
                    for dc in range(NDC):
                        w = []
                        if dc == 0:
                            w = g_wait(f"pbank{bank}") + guard["wk_dma"]
                            if g == 0 and hh == 0:
                                w.append(("V", V_ln_chunk[4 * tb + 3]))
                        T_k = emit("tensor",
                                   lambda e, _b=bank, _dc=dc, _hh=hh, _tb=tb: e.matmul(
                                       pf(_b)[:, :],
                                       wk_s[:, _dc * 256 + _hh * 128:_dc * 256 + (_hh + 1) * 128],
                                       xlnT3[:, _dc, _tb * 512:(_tb + 1) * 512],
                                       start=(_dc == 0), stop=(_dc == NDC - 1)),
                                   waits=w, inc="T")
                    hcol = 2 * g + hh
                    S_ev = emit("scalar",
                                lambda e, _b=bank, _hh=hh, _tb=tb, _hc=hcol: e.activation(
                                    out=kT_g[:, _hh * T + _tb * 512:_hh * T + (_tb + 1) * 512],
                                    in_=pf(_b)[:, :], func=AF.Identity,
                                    bias=bk_s[:, _hc:_hc + 1]),
                                waits=[("T", T_k)] +
                                      (g_wait("kT_rd") if tb == 0 and hh == 0 else []) +
                                      ([("Lc", L_const)] if g == 0 and hh == 0 and tb == 0
                                       else []),
                                inc="S")
                    g_set(f"pbank{bank}", "S", S_ev)
            for hh in range(2):
                for qb in range(2):
                    bank = (hh * 2 + qb) % 2
                    T_q = None
                    for dc in range(NDC):
                        w = []
                        if dc == 0:
                            w = g_wait(f"pbank{bank}") + guard["wq_dma"]
                        T_q = emit("tensor",
                                   lambda e, _b=bank, _dc=dc, _hh=hh, _qb=qb: e.matmul(
                                       pf(_b)[:, :],
                                       wq_s[:, _dc * 256 + _hh * 128:_dc * 256 + (_hh + 1) * 128],
                                       xlnT3[:, _dc, _qb * 512:(_qb + 1) * 512],
                                       start=(_dc == 0), stop=(_dc == NDC - 1)),
                                   waits=w, inc="T")
                    hcol = 2 * g + hh
                    S_ev = emit("scalar",
                                lambda e, _b=bank, _hh=hh, _qb=qb, _hc=hcol: e.activation(
                                    out=qT_g[:, _hh * TQ + _qb * 512:_hh * TQ + (_qb + 1) * 512],
                                    in_=pf(_b)[:, :], func=AF.Identity,
                                    bias=bq_s[:, _hc:_hc + 1]),
                                waits=[("T", T_q)] +
                                      (g_wait("qT_rd") if hh == 0 and qb == 0 else []),
                                inc="S")
                    g_set(f"pbank{bank}", "S", S_ev)
            for tc in range(NTC):
                bank = tc % 2
                T_v = None
                for dc in range(NDC):
                    w = []
                    if dc == 0:
                        w = g_wait(f"pbank{bank}") + guard["wv_dma"]
                    T_v = emit("tensor",
                               lambda e, _b=bank, _dc=dc, _tc=tc: e.matmul(
                                   pf(_b)[:, 0:256],
                                   xlnT3[:, _dc, _tc * 128:(_tc + 1) * 128],
                                   wv_s[:, _dc * 256:(_dc + 1) * 256],
                                   start=(_dc == 0), stop=(_dc == NDC - 1)),
                               waits=w, inc="T")
                V_ev = emit("vector",
                            lambda e, _b=bank, _tc=tc, _g=g: e.tensor_tensor(
                                out=v_g[:, _tc * 256:(_tc + 1) * 256],
                                in0=pf(_b)[:, 0:256],
                                in1=bvb_s[:, _g * 256:(_g + 1) * 256], op=ALU.add),
                            waits=[("T", T_v)] +
                                  ([("Wc", W_bvb)] if g == 0 and tc == 0 else []) +
                                  (g_wait("v_rd") if tc == 0 else []),
                            inc="V")
                g_set(f"pbank{bank}", "V", V_ev)
            for nm in ("wq_s", "wk_s", "wv_s"):
                g_set(nm, "T", cnt["T"])
            ch_set = [set(attn_chunks(j)) for j in range(2)]
            for hh in range(2):
                head = 2 * g + hh
                ready = {}
                T_rs_last = {}
                T_av_last = {}
                pending = []

                def emit_rs_av(c, _hh=hh):
                    for j in (0, 1):
                        if c not in ch_set[j]:
                            continue
                        first = c == 0
                        last = c == 15
                        slot = c % 8
                        wrs = [ready[(c, j)]]
                        if first:
                            wrs += g_wait(f"pbank{4 + j}") + [("V", V_ones)]
                        T_rs_last[j] = emit(
                            "tensor",
                            lambda e, _c=c, _j=j, _s=slot, _f=first, _l=last: e.matmul(
                                PF[2][:, _j * 512:(_j + 1) * 512], ones_sq[:, :],
                                pt[:, _s * 1024 + _j * 512:_s * 1024 + (_j + 1) * 512],
                                start=_f, stop=_l),
                            waits=wrs, inc="T")
                        wav = g_wait(f"pbank{6 + j}") if first else []
                        T_av_last[j] = emit(
                            "tensor",
                            lambda e, _c=c, _j=j, _s=slot, _f=first, _l=last, _h=_hh: e.matmul(
                                PF[3][:, _j * 512:(_j + 1) * 512],
                                v_g[:, _c * 256 + _h * 128:_c * 256 + (_h + 1) * 128],
                                pt[:, _s * 1024 + _j * 512:_s * 1024 + (_j + 1) * 512],
                                start=_f, stop=_l),
                            waits=wav, inc="T")
                    pt_guard[c % 8] = [("T", max(T_av_last.values()))]

                for c in range(16):
                    if c == 2 and deferred_tail[0] is not None:
                        deferred_tail[0]()
                        deferred_tail[0] = None
                    J = [j for j in (0, 1) if c in ch_set[j]]
                    T_lg = 0
                    for j in J:
                        lb = 2 * (c % 2) + j
                        T_lg = max(T_lg, emit(
                            "tensor",
                            lambda e, _lb=lb, _c=c, _hh=hh, _j=j: e.matmul(
                                pf(_lb)[:, :],
                                kT_g[:, _hh * T + _c * 128:_hh * T + (_c + 1) * 128],
                                qT_g[:, _hh * TQ + _j * 512:_hh * TQ + (_j + 1) * 512],
                                start=True, stop=True),
                            waits=g_wait(f"pbank{lb}"), inc="T"))
                    slot = c % 8
                    wexp = [("T", T_lg)] + list(pt_guard[slot])
                    if J == [0, 1]:
                        S_ex = emit("scalar",
                                    lambda e, _c=c, _s=slot: e.activation(
                                        out=pt[:, _s * 1024:(_s + 1) * 1024],
                                        in_=PF[_c % 2][:, :], func=AF.Exp, scale=SCALE),
                                    waits=wexp, inc="S")
                    else:
                        S_ex = emit("scalar",
                                    lambda e, _c=c, _s=slot: e.activation(
                                        out=pt[:, _s * 1024 + 512:_s * 1024 + 1024],
                                        in_=PF[_c % 2][:, 512:1024], func=AF.Exp, scale=SCALE),
                                    waits=wexp, inc="S")
                    for j in J:
                        g_set(f"pbank{2 * (c % 2) + j}", "S", S_ex)
                    mask_j = [j for j in J if c >= 4 * j]
                    wmk = [("S", S_ex)] + ([("Lm", L_mask)] if g == 0 and hh == 0 and c in (0, 8) else [])
                    if mask_j == [0, 1]:
                        V_mk = emit("vector",
                                    lambda e, _c=c, _s=slot: e.tensor_tensor(
                                        out=pt[:, _s * 1024:(_s + 1) * 1024],
                                        in0=pt[:, _s * 1024:(_s + 1) * 1024],
                                        in1=mask3[:, min(_c, 8), :], op=ALU.mult),
                                    waits=wmk, inc="V")
                        for j in J:
                            ready[(c, j)] = ("V", V_mk)
                        pending.append(c)
                        if len(pending) >= 3:
                            emit_rs_av(pending.pop(0))
                        continue
                    else:
                        mj = mask_j[0]
                        V_mk = emit("vector",
                                    lambda e, _c=c, _s=slot, _j=mj: e.tensor_tensor(
                                        out=pt[:, _s * 1024 + _j * 512:_s * 1024 + (_j + 1) * 512],
                                        in0=pt[:, _s * 1024 + _j * 512:_s * 1024 + (_j + 1) * 512],
                                        in1=mask3[:, min(_c, 8), _j * 512:(_j + 1) * 512],
                                        op=ALU.mult),
                                    waits=wmk, inc="V")
                    for j in J:
                        ready[(c, j)] = ("V", V_mk) if j in mask_j else ("S", S_ex)
                    pending.append(c)
                    if len(pending) >= 3:
                        emit_rs_av(pending.pop(0))
                while pending:
                    emit_rs_av(pending.pop(0))

                def make_tail(_head, _rs, _av):
                    def tail():
                        for j in range(2):
                            V_rc = emit("vector",
                                        lambda e, _j=j: e.reciprocal(
                                            rb_sb[:, :],
                                            PF[2][:, _j * 512:(_j + 1) * 512]),
                                        waits=[("T", _rs[j])] + g_wait("rb_sb"), inc="V")
                            g_set(f"pbank{4 + j}", "V", V_rc)
                            V_at = emit("vector",
                                        lambda e, _j=j, _hc=_head: e.tensor_tensor(
                                            out=attnT[:, _hc * TQ + _j * 512:
                                                      _hc * TQ + (_j + 1) * 512],
                                            in0=PF[3][:, _j * 512:(_j + 1) * 512],
                                            in1=rb_sb[:, :], op=ALU.mult),
                                        waits=[("T", _av[j])], inc="V")
                            g_set(f"pbank{6 + j}", "V", V_at)
                            g_set("rb_sb", "V", V_at)
                    return tail

                deferred_tail[0] = make_tail(head, dict(T_rs_last), dict(T_av_last))
            for nm in ("kT_rd", "qT_rd", "v_rd"):
                g_set(nm, "T", cnt["T"])
        if deferred_tail[0] is not None:
            deferred_tail[0]()
            deferred_tail[0] = None
        V_attn_done = cnt["V"]

        ar.free("mask"); ar.free("wq_s"); ar.free("wk_s"); ar.free("wv_s")
        ar.free("kT_g"); ar.free("qT_g"); ar.free("v_g"); ar.free("pt")
        ar.free("xlnT"); ar.free("bvb"); ar.free("recip"); ar.free("rb")

        preset(["oT_w", "x1_w", "xres0", "xres1", "xln0", "xln1", "xln2T_w"])
        ar.alloc("oT", NDC * TQ * 2, side="top")
        oT = aview("oT", BF16, NDC * TQ)
        wo_src = wo_h.rearrange("(hc p) n -> p hc n", p=128)

        ar.alloc("x1", NQC * D * 4)
        x1 = aview("x1", F32, NQC * D)
        ar.alloc("xres0", D * 4, side="top"); ar.alloc("xres1", D * 4, side="top")
        xres = [aview("xres0", F32, D), aview("xres1", F32, D)]
        V_x1 = [None] * NQC

        ar.alloc("xln2T", NDC * TQ * 2)
        xln2T = aview("xln2T", BF16, NDC * TQ)
        xln2T3 = xln2T.rearrange("p (c t) -> p c t", c=NDC)
        ar.alloc("xln0", D * 2, side="top"); ar.alloc("xln1", D * 2, side="top")
        xln2 = [aview("xln0", BF16, D), aview("xln1", BF16, D)]
        vvar_d = {}
        V_ln2_chunk = {}

        def ln2_feed(tc):
            vvar_d[tc] = emit_ln_stage1(tc, x1[:, tc * D:(tc + 1) * D], st2, tc,
                                        xln2[tc % 2], f"xln{tc % 2}",
                                        [("V", V_x1[tc])])
            if tc > 0:
                V_ln2_chunk[tc - 1] = emit_ln_stage2(
                    tc - 1, x1[:, (tc - 1) * D:tc * D], st2, tc - 1,
                    xln2[(tc - 1) % 2], xln2T3, f"xln{(tc - 1) % 2}",
                    vvar_d[tc - 1], dst_guard="xln2T_w")
        wo_src2 = wo_h.rearrange("(hc p) n -> p hc n", p=128)
        S_oev = {}
        for j in range(2):
            for doc in range(16):
                buf = (j * 16 + doc) % 4
                W_wo = emit("gpsimd",
                            lambda e, _b=buf, _doc=doc: e.dma_start(
                                out=wo_sv[_b][:, :].rearrange("p (hc n) -> p hc n", hc=NDC),
                                in_=wo_src2[:, :, _doc * 128:(_doc + 1) * 128]),
                            waits=g_wait(f"wo{buf}"), inc=f"Wo{buf}", n=16)
                bank = doc % 2
                T_o = None
                for hc in range(NDC):
                    w = []
                    if hc == 0:
                        w = g_wait(f"pbank{bank}") + [(f"Wo{buf}", W_wo)]
                        if doc == 0 and j == 0:
                            w.append(("V", V_attn_done))
                    T_o = emit("tensor",
                               lambda e, _b=bank, _hc=hc, _j=j, _buf=buf: e.matmul(
                                   pf(_b)[:, :],
                                   wo_sv[_buf][:, _hc * 128:(_hc + 1) * 128],
                                   attnT[:, _hc * TQ + _j * 512:_hc * TQ + (_j + 1) * 512],
                                   start=(_hc == 0), stop=(_hc == NDC - 1)),
                               waits=w, inc="T")
                g_set(f"wo{buf}", "T", T_o)
                S_oev[(j, doc)] = emit(
                    "scalar",
                    lambda e, _b=bank, _doc=doc, _j=j: e.activation(
                        out=oT[:, _doc * TQ + _j * 512:_doc * TQ + (_j + 1) * 512],
                        in_=pf(_b)[:, :], func=AF.Identity, bias=bo_s[:, _doc:_doc + 1]),
                    waits=[("T", T_o)] + (g_wait("oT_w") if doc == 0 and j == 0 else []),
                    inc="S")
                g_set(f"pbank{bank}", "S", S_oev[(j, doc)])
                if j == 1 and doc % 4 == 3:
                    ln2_feed(doc // 4)
            for tc in range(j * 4, j * 4 + 4):
                L_xr = emit("sync",
                            lambda e, _tc=tc: e.dma_start(
                                out=xres[_tc % 2][:, :],
                                in_=x_h[_tc * 128:(_tc + 1) * 128, :]),
                            waits=g_wait(f"xres{tc % 2}"), inc=f"Lxe{tc % 2}", n=16)
                T_tr = None
                S_need = S_oev[(j, 15)]
                for doc in range(16):
                    bank = 6 + doc // 8
                    slot = doc % 8
                    w = [("S", S_need)] if doc == 0 else []
                    if doc in (0, 8):
                        w += g_wait(f"pbank{bank}")
                    T_tr = emit("tensor",
                                lambda e, _b=bank, _s=slot, _doc=doc, _tc=tc: e.transpose(
                                    pbf(_b)[:, _s * 128:(_s + 1) * 128],
                                    oT[:, _doc * TQ + _tc * 128:_doc * TQ + (_tc + 1) * 128],
                                    identb[:, :]),
                                waits=w, inc="T")
                V_as = None
                for dseg in range(4):
                    bank = 6 + dseg // 2
                    off = (dseg % 2) * 512
                    V_as = emit("vector",
                                lambda e, _b=bank, _o=off, _tc=tc, _d=dseg: e.tensor_tensor(
                                    out=x1[:, _tc * D + _d * 512:_tc * D + (_d + 1) * 512],
                                    in0=xres[_tc % 2][:, _d * 512:(_d + 1) * 512],
                                    in1=pbf(_b)[:, _o:_o + 512], op=ALU.add),
                                waits=[("T", T_tr), (f"Lxe{tc % 2}", L_xr)] +
                                      (g_wait("x1_w") if tc == 0 and dseg == 0 else []),
                                inc="V")
                g_set("pbank6", "V", V_as)
                g_set("pbank7", "V", V_as)
                g_set(f"xres{tc % 2}", "V", V_as)
                V_x1[tc] = V_as
                emit("sync",
                     lambda e, _tc=tc: e.dma_start(
                         out=x1_hbm[_tc * 128:(_tc + 1) * 128, :],
                         in_=x1[:, _tc * D:(_tc + 1) * D]),
                     waits=[("V", V_as)], inc="SX", n=16)
                if j == 1:
                    ln2_feed(tc)
        ST_x1 = cnt["SX"]
        ar.free("wo_s"); ar.free("oT"); ar.free("attnT")
        ar.free("xres0"); ar.free("xres1")

        preset(["w1b0", "w1b1", "w1b2", "w3b0", "w3b1", "w3b2", "silu0", "silu1"])
        ar.alloc("w1_s", NDC * 128 * 2 * 3); ar.alloc("w3_s", NDC * 128 * 2 * 3)
        ar.alloc("silu0", TQ * 2); ar.alloc("silu1", TQ * 2)
        for _a in ("w1_s", "w3_s", "silu0", "silu1"):
            _s0, _n0 = ar.allocs[_a]
            for _b in ("x1", "xln0", "xln1", "xln2T"):
                _s1, _n1 = ar.allocs[_b]
                assert _s0 + _n0 <= _s1 or _s1 + _n1 <= _s0, (_a, _b)
        w1_all = aview("w1_s", BF16, NDC * 128 * 3)
        w3_all = aview("w3_s", BF16, NDC * 128 * 3)
        w1_sv = [w1_all[:, i * NDC * 128:(i + 1) * NDC * 128] for i in range(3)]
        w3_sv = [w3_all[:, i * NDC * 128:(i + 1) * NDC * 128] for i in range(3)]
        silu_sb = [aview("silu0", BF16, TQ), aview("silu1", BF16, TQ)]
        w1_src = w1t_h.rearrange("(dc p) n -> p dc n", p=128)
        w3_src = w3t_h.rearrange("(dc p) n -> p dc n", p=128)

        V_ln2_chunk[NQC - 1] = emit_ln_stage2(
            NQC - 1, x1[:, (NQC - 1) * D:NQC * D], st2, NQC - 1,
            xln2[(NQC - 1) % 2], xln2T3, f"xln{(NQC - 1) % 2}",
            vvar_d[NQC - 1], dst_guard="xln2T_w")
        ar.free("xln0"); ar.free("xln1")
        ar.free("x1")

        guard["hid_w"] = [("V", V_ln2_chunk[NQC - 1]), ("SX", ST_x1)]
        for _s in range(NHB + 1):
            ar.alloc(f"hid{_s}", TQ * 2)
        _hviews = [aview(f"hid{_s}", BF16, TQ) for _s in range(NHB + 1)]

        def hslot(s, lo, hi):
            return _hviews[s][:, lo:hi]

        loc = list(range(NHB))
        bf_state = {"spare": NHB}
        slot_last_v = {}

        def bfly_rec(base, size):
            if size == 2:
                return [(base, 1)]
            h = size // 2
            return (bfly_rec(base, h) + bfly_rec(base + h, h) +
                    [(i, h) for i in range(base, base + h)])

        def emit_one_bfly(i, dd):
            a_s, b_s = loc[i], loc[i + dd]
            f_s = bf_state["spare"]
            V1 = emit("vector",
                      lambda e, _a=a_s, _b=b_s, _f=f_s: e.tensor_tensor(
                          out=hslot(_f, 0, TQ), in0=hslot(_a, 0, TQ),
                          in1=hslot(_b, 0, TQ), op=ALU.add),
                      inc="V")
            V2 = emit("vector",
                      lambda e, _a=a_s, _b=b_s: e.tensor_tensor(
                          out=hslot(_a, 0, TQ), in0=hslot(_a, 0, TQ),
                          in1=hslot(_b, 0, TQ), op=ALU.subtract),
                      inc="V")
            slot_last_v[f_s] = V1
            slot_last_v[a_s] = V2
            loc[i], loc[i + dd], bf_state["spare"] = f_s, a_s, b_s

        bf_lower = bfly_rec(0, 32)
        bf_upper = bfly_rec(32, 32)

        for hb in range(NHB):
            buf = hb % 2
            wbuf = hb % 3
            for nm, dst, srcv in (("w1", w1_sv[wbuf], w1_src), ("w3", w3_sv[wbuf], w3_src)):
                W_s = emit("gpsimd",
                           lambda e, _d=dst, _s=srcv, _hb=hb: e.dma_start(
                               out=_d[:, :].rearrange("p (dc n) -> p dc n", dc=NDC),
                               in_=_s[:, :, _hb * 128:(_hb + 1) * 128]),
                           waits=g_wait(f"{nm}b{wbuf}"), inc=f"W{nm}{wbuf}", n=16)
                guard[f"{nm}_dma"] = [(f"W{nm}{wbuf}", W_s)]
            for qb in range(2):
                gate_b = (hb % 2) * 4 + qb
                lin_b = (hb % 2) * 4 + 2 + qb
                T_gate = T_lin = None
                for dc in range(NDC):
                    wg = []
                    wl = []
                    if dc == 0:
                        wg = g_wait(f"pbank{gate_b}") + guard["w1_dma"]
                        wl = g_wait(f"pbank{lin_b}") + guard["w3_dma"]
                        if hb == 0:
                            ln_w = ("V", V_ln2_chunk[4 * qb + 3])
                            wg.append(ln_w)
                            wl.append(ln_w)
                    T_gate = emit("tensor",
                                  lambda e, _b=gate_b, _dc=dc, _qb=qb, _buf=wbuf: e.matmul(
                                      pf(_b)[:, :],
                                      w1_sv[_buf][:, _dc * 128:(_dc + 1) * 128],
                                      xln2T3[:, _dc, _qb * 512:(_qb + 1) * 512],
                                      start=(_dc == 0), stop=(_dc == NDC - 1)),
                                  waits=wg, inc="T")
                    T_lin = emit("tensor",
                                 lambda e, _b=lin_b, _dc=dc, _qb=qb, _buf=wbuf: e.matmul(
                                     pf(_b)[:, :],
                                     w3_sv[_buf][:, _dc * 128:(_dc + 1) * 128],
                                     xln2T3[:, _dc, _qb * 512:(_qb + 1) * 512],
                                     start=(_dc == 0), stop=(_dc == NDC - 1)),
                                 waits=wl, inc="T")
                g_set(f"w1b{wbuf}", "T", T_gate)
                g_set(f"w3b{wbuf}", "T", T_lin)
                S_si = emit("scalar",
                            lambda e, _b=gate_b, _buf=buf, _qb=qb: e.activation(
                                out=silu_sb[_buf][:, _qb * 512:(_qb + 1) * 512],
                                in_=pf(_b)[:, :], func=AF.Silu),
                            waits=[("T", T_gate)] +
                                  (g_wait(f"silu{buf}") if qb == 0 else []),
                            inc="S")
                g_set(f"pbank{gate_b}", "S", S_si)
                V_h = emit("vector",
                           lambda e, _b=lin_b, _buf=buf, _qb=qb, _hb=hb: e.tensor_tensor(
                               out=hslot(_hb, _qb * 512, (_qb + 1) * 512),
                               in0=silu_sb[_buf][:, _qb * 512:(_qb + 1) * 512],
                               in1=pf(_b)[:, :], op=ALU.mult),
                           waits=[("S", S_si), ("T", T_lin)] +
                                 (g_wait("hid_w") if hb == 0 and qb == 0 else []),
                           inc="V")
                g_set(f"pbank{lin_b}", "V", V_h)
                if qb == 1:
                    g_set(f"silu{buf}", "V", V_h)
            if hb >= 32:
                emit_one_bfly(hb - 32, 32)
            drained = 0
            while drained < 6:
                if bf_lower and bf_lower[0][0] + bf_lower[0][1] + 32 <= hb:
                    emit_one_bfly(*bf_lower.pop(0)); drained += 1; continue
                if bf_upper and bf_upper[0][0] + bf_upper[0][1] <= hb:
                    emit_one_bfly(*bf_upper.pop(0)); drained += 1; continue
                break
        for lst in (bf_lower, bf_upper):
            while lst:
                emit_one_bfly(*lst.pop(0))
        V_h64_done = cnt["V"]
        ar.free("w1_s"); ar.free("w3_s"); ar.free("silu0"); ar.free("silu1")
        ar.free("xln2T")

        preset(["colh0", "colh1", "colh2", "colh3"])
        ar.alloc("colbuf", 4 * 32 * 128 * 2, side="top")
        colb_all = aview("colbuf", BF16, 4 * 32 * 128)
        colh = [colb_all[:, i * 32 * 128:(i + 1) * 32 * 128] for i in range(4)]
        w2t_src = w2t_h.rearrange("(hb p) n -> p hb n", p=128)

        h128_order = sorted(range(NHB), key=lambda i: slot_last_v.get(loc[i], 0))
        S_h128 = None
        S_h128_chunk = {}
        for pos, i in enumerate(h128_order):
            slot = loc[i]
            T_h1 = None
            for half in range(2):
                bank = 2 * (pos % 2) + half
                T_h1 = emit("tensor",
                            lambda e, _p=pos, _s=slot, _h=half: e.matmul(
                                PF[_p % 2][:, _h * 512:(_h + 1) * 512], h128_s[:, :],
                                hslot(_s, _h * 512, (_h + 1) * 512),
                                start=True, stop=True),
                            waits=g_wait(f"pbank{bank}") +
                                  ([("V", slot_last_v.get(slot, V_h64_done))]
                                   if half == 0 else []),
                            inc="T")
            S_h128 = emit("scalar",
                          lambda e, _p=pos, _s=slot: e.activation(
                              out=hslot(_s, 0, TQ), in_=PF[_p % 2][:, :],
                              func=AF.Identity, scale=FW_SCALE),
                          waits=[("T", T_h1)], inc="S")
            for half in range(2):
                g_set(f"pbank{2 * (pos % 2) + half}", "S", S_h128)
            S_h128_chunk[i] = S_h128
        S_h128_done = S_h128

        preset(["o2T", "outb0", "outb1"])
        ar.alloc("o2T", 16 * 512 * 2)
        o2T = aview("o2T", BF16, 16 * 512)
        ar.alloc("outb0", D * 4, side="top"); ar.alloc("outb1", D * 4, side="top")
        out_sb = [aview("outb0", F32, D), aview("outb1", F32, D)]

        for qb in range(2):
            S_ev_last = None
            for doc in range(16):
                L_half = [None, None]
                for half in range(2):
                    hbuf = (2 * doc + half) % 4
                    L_half[half] = emit(
                        "gpsimd",
                        lambda e, _hb=hbuf, _doc=doc, _h=half: e.dma_start(
                            out=colh[_hb][:, :].rearrange("p (hb t) -> p hb t", hb=32),
                            in_=w2t_src[:, _h * 32:(_h + 1) * 32,
                                        _doc * 128:(_doc + 1) * 128]),
                        waits=g_wait(f"colh{hbuf}"), inc=f"Lch{hbuf}", n=16)
                bank = doc % 2
                T_d = None
                for hb in range(NHB):
                    w = []
                    if hb == 0:
                        w = g_wait(f"pbank{bank}") + [(f"Lch{(2 * doc) % 4}", L_half[0]),
                                                       ("S", max(S_h128_chunk[i]
                                                                 for i in range(32)))]
                    elif hb == 32:
                        w = [(f"Lch{(2 * doc + 1) % 4}", L_half[1]),
                             ("S", max(S_h128_chunk[i] for i in range(NHB)))]
                    slot = loc[hb]
                    hbuf = (2 * doc + hb // 32) % 4
                    T_d = emit("tensor",
                               lambda e, _b=bank, _hb=hb, _buf=hbuf, _s=slot, _qb=qb: e.matmul(
                                   pf(_b)[:, :],
                                   colh[_buf][:, (_hb % 32) * 128:((_hb % 32) + 1) * 128],
                                   hslot(_s, _qb * 512, (_qb + 1) * 512),
                                   start=(_hb == 0), stop=(_hb == NHB - 1)),
                               waits=w, inc="T")
                    if hb == 31:
                        g_set(f"colh{(2 * doc) % 4}", "T", cnt["T"])
                g_set(f"colh{(2 * doc + 1) % 4}", "T", T_d)
                S_ev_last = emit("scalar",
                                 lambda e, _b=bank, _doc=doc: e.activation(
                                     out=o2T[:, _doc * 512:(_doc + 1) * 512],
                                     in_=pf(_b)[:, :], func=AF.Copy),
                                 waits=[("T", T_d)] + (g_wait("o2T") if doc == 0 else []),
                                 inc="S")
                g_set(f"pbank{bank}", "S", S_ev_last)
            T_tr_last = None
            for tc in range(4):
                buf = tc % 2
                L_x1 = emit("sync",
                            lambda e, _b=buf, _tc=tc, _qb=qb: e.dma_start(
                                out=out_sb[_b][:, :],
                                in_=x1_hbm[_qb * 512 + _tc * 128:
                                           _qb * 512 + (_tc + 1) * 128, :]),
                            waits=[("SX", ST_x1)] + g_wait(f"outb{buf}"), inc=f"Lo{buf}", n=16)
                T_tr = None
                for doc in range(16):
                    bank = 6 + doc // 8
                    slot = doc % 8
                    w = [("S", S_ev_last)] if doc == 0 and tc == 0 else []
                    if doc in (0, 8):
                        w += g_wait(f"pbank{bank}")
                    T_tr = emit("tensor",
                                lambda e, _b=bank, _s=slot, _doc=doc, _tc=tc: e.transpose(
                                    pbf(_b)[:, _s * 128:(_s + 1) * 128],
                                    o2T[:, _doc * 512 + _tc * 128:_doc * 512 + (_tc + 1) * 128],
                                    identb[:, :]),
                                waits=w, inc="T")
                T_tr_last = T_tr
                V_ad = None
                for dseg in range(4):
                    bank = 6 + dseg // 2
                    off = (dseg % 2) * 512
                    V_ad = emit("vector",
                                lambda e, _b=bank, _o=off, _d=dseg, _buf=buf: e.tensor_tensor(
                                    out=out_sb[_buf][:, _d * 512:(_d + 1) * 512],
                                    in0=out_sb[_buf][:, _d * 512:(_d + 1) * 512],
                                    in1=pbf(_b)[:, _o:_o + 512], op=ALU.add),
                                waits=[("T", T_tr), (f"Lo{buf}", L_x1)], inc="V")
                g_set("pbank6", "V", V_ad)
                g_set("pbank7", "V", V_ad)
                ST_o = emit("sync",
                            lambda e, _b=buf, _tc=tc, _qb=qb: e.dma_start(
                                out=out_h[_qb * 512 + _tc * 128:
                                          _qb * 512 + (_tc + 1) * 128, :],
                                in_=out_sb[_b][:, :]),
                            waits=[("V", V_ad)], inc=f"SO{buf}", n=16)
                g_set(f"outb{buf}", f"SO{buf}", ST_o)
            g_set("o2T", "T", T_tr_last)

        emit("sync", lambda e: None,
             waits=[("SO0", cnt["SO0"]), ("SO1", cnt["SO1"])])

        assert ar.hwm <= AR_BYTES, ar.hwm
        build_nc.hwm = ar.hwm

        import os as _os
        lazy_on = _os.environ.get("KLAZY", "1") == "1"
        needed_t = set()
        for _e in ENGINES:
            for _w, _b, _i, _n in ops[_e]:
                for _s, _v in _w:
                    if _s == "T":
                        needed_t.add(_v)
        new_tops = []
        run_v = 0
        last_v = 0
        for _w, _b, _i, _n in ops["tensor"]:
            if _i == "T":
                run_v += _n
                if run_v in needed_t:
                    new_tops.append((_w, _b, "T", run_v - last_v))
                    last_v = run_v
                else:
                    new_tops.append((_w, _b, None, 0))
            else:
                new_tops.append((_w, _b, _i, _n))
        if run_v != last_v:
            for idx in range(len(new_tops) - 1, -1, -1):
                if new_tops[idx][2] is None and ops["tensor"][idx][2] == "T":
                    new_tops[idx] = (new_tops[idx][0], new_tops[idx][1], "T", run_v - last_v)
                    break
        if lazy_on:
            ops["tensor"] = new_tops

        block = es.enter_context(nc.Block())

        def mk_runner(eng_ops):
            def run(e):
                for waits, build, inc, n in eng_ops:
                    for s, v in waits:
                        e.wait_ge(get_sem(s), v)
                    ins = build(e)
                    if inc is not None:
                        ins.then_inc(get_sem(inc), n)
            return run

        block.tensor(mk_runner(ops["tensor"]))
        block.vector(mk_runner(ops["vector"]))
        block.scalar(mk_runner(ops["scalar"]))
        block.gpsimd(mk_runner(ops["gpsimd"]))
        block.sync(mk_runner(ops["sync"]))

    return nc


_NC_CACHE = None


def _get_nc():
    global _NC_CACHE
    if _NC_CACHE is None:
        _NC_CACHE = build_nc()
    return _NC_CACHE


def _hadamard(n):
    h = np.array([[1.0]], dtype=np.float32)
    while h.shape[0] < n:
        h = np.block([[h, h], [h, -h]])
    return h.astype(np.float32)


def prep_in_maps(L0_attn_query, L0_attn_w_q, L0_attn_b_q, L0_attn_w_k, L0_attn_b_k,
                 L0_attn_w_v, L0_attn_b_v, L0_attn_w_o, L0_attn_b_o,
                 L0_ffn_W1, L0_ffn_W2, L0_ffn_W3):
    bf = ml_dtypes.bfloat16
    x = np.ascontiguousarray(np.asarray(L0_attn_query, dtype=np.float32))
    wq = np.ascontiguousarray(np.asarray(L0_attn_w_q, dtype=np.float32))
    wk = np.ascontiguousarray(np.asarray(L0_attn_w_k, dtype=np.float32))
    wv = np.ascontiguousarray(np.asarray(L0_attn_w_v, dtype=np.float32))
    wo = np.ascontiguousarray(np.asarray(L0_attn_w_o, dtype=np.float32))
    w1t = np.ascontiguousarray(np.asarray(L0_ffn_W1, dtype=np.float32).T)
    w3t = np.ascontiguousarray(np.asarray(L0_ffn_W3, dtype=np.float32).T)
    w2t = np.ascontiguousarray(np.asarray(L0_ffn_W2, dtype=np.float32).T)
    bq = np.ascontiguousarray(np.asarray(L0_attn_b_q, dtype=np.float32).reshape(16, 128).T)
    bk = np.ascontiguousarray(np.asarray(L0_attn_b_k, dtype=np.float32).reshape(16, 128).T)
    bo = np.ascontiguousarray(np.asarray(L0_attn_b_o, dtype=np.float32).reshape(16, 128).T)
    bvb = np.ascontiguousarray(np.broadcast_to(
        np.asarray(L0_attn_b_v, dtype=np.float32).reshape(1, D), (128, D)))
    h128 = _hadamard(128).astype(bf)
    identb = np.eye(128, dtype=np.float32).astype(bf)

    masks = []
    for half in range(2):
        q = half * TQ + np.arange(TQ)[None, :]
        k = (np.arange(9 * 128)[:, None] + half * TQ) % T
        m = (q >= k).astype(np.float32)
        masks.append(np.ascontiguousarray(m.astype(bf)))

    in_maps = []
    for c in range(8):
        b, half = c // 2, c % 2
        xc = np.ascontiguousarray(np.roll(x[b], -half * TQ, axis=0))
        in_maps.append(dict(
            x=xc, wq=wq, wk=wk, wv=wv, wo=wo, bq=bq, bk=bk, bo=bo, bvb=bvb,
            w1t=w1t, w3t=w3t, w2t=w2t, maskt=masks[half], h128=h128, identb=identb,
        ))
    return in_maps


def kernel(**inputs):
    in_maps = prep_in_maps(**inputs)
    nc = _get_nc()
    res = run_bass_kernel_spmd(nc, in_maps, core_ids=list(range(8)))
    out = np.empty((4, T, D), dtype=np.float32)
    for c in range(8):
        b, half = c // 2, c % 2
        out[b, half * TQ:(half + 1) * TQ] = res.results[c]["out"]
    return out
